# version 12
# baseline (speedup 1.0000x reference)
import numpy as np
import ml_dtypes
import concourse.bass as bass
import concourse.mybir as mybir
from concourse.bass_utils import run_bass_kernel_spmd

F32 = mybir.dt.float32
BF16 = mybir.dt.bfloat16
I32 = mybir.dt.int32
AF = mybir.ActivationFunctionType
ALU = mybir.AluOpType
BF = ml_dtypes.bfloat16

D = 1024
S = 4096
NOWN = 16
NGRP = 8
SB_BASE = 16512
PBYTES = 211328

C_NHP = 8
C_NGRP = 8
C_CUT = set()
FUSED = True
LAUNCH_PLAN = ([["A", "B", "C", "W", "D", "E", "F", "G"]] if FUSED
               else [["A"], ["B"], ["C"], ["W", "D"], ["E"], ["F", "G"]])


def own_global_tiles(sub):
    return [4 * i + 2 * sub + j for i in range(8) for j in range(2)]


def slot_global(s, sub):
    u = s // 2
    if s % 2 == 0:
        return 4 * (u // 2) + 2 * sub + (u % 2)
    return 4 * (u // 2) + 2 * (1 - sub) + (u % 2)


def build_masks(sub):
    masks = np.zeros((NGRP, 4, 128, 256), np.float32)
    for i in range(NGRP):
        gq = [4 * i + 2 * sub, 4 * i + 2 * sub + 1]
        qt_g = np.concatenate([g * 128 + np.arange(128) for g in gq])[None, :]
        for kk in range(4):
            gk = slot_global(4 * i + kk, sub)
            kt_g = gk * 128 + np.arange(128)[:, None]
            masks[i, kk] = (kt_g <= qt_g).astype(np.float32)
    return masks.astype(BF)


def host_shards(inputs):
    x = np.ascontiguousarray(np.asarray(inputs["L0_attn_x"], dtype=np.float32))
    ident = np.eye(128).astype(BF)
    tt = (np.arange(128)[:, None] // 16) == (np.arange(512)[None, :] // 64)
    vblkmask = tt.astype(np.float32).astype(BF)
    dd = (np.arange(128)[:, None] // 16) == (np.arange(128)[None, :] // 16)
    diagmask = dd.astype(np.float32).astype(BF)
    shared = dict(
        Wq=np.ascontiguousarray(np.asarray(inputs["L0_attn_Wq"], np.float32)),
        Wk=np.ascontiguousarray(np.asarray(inputs["L0_attn_Wk"], np.float32)),
        Wv=np.ascontiguousarray(np.asarray(inputs["L0_attn_Wv"], np.float32)),
        Wo=np.ascontiguousarray(np.asarray(inputs["L0_attn_Wo"], np.float32)),
        lq1=np.asarray(inputs["L0_attn_lq1"], np.float32).reshape(1, 64).copy(),
        lk1=np.asarray(inputs["L0_attn_lk1"], np.float32).reshape(1, 64).copy(),
        lq2=np.asarray(inputs["L0_attn_lq2"], np.float32).reshape(1, 64).copy(),
        lk2=np.asarray(inputs["L0_attn_lk2"], np.float32).reshape(1, 64).copy(),
        Ws1=np.ascontiguousarray(np.asarray(inputs["L0_ffn_Ws1"], np.float32)),
        Ws2=np.ascontiguousarray(np.asarray(inputs["L0_ffn_Ws2"], np.float32)),
        W1=np.ascontiguousarray(np.asarray(inputs["L0_ffn_W1"], np.float32)),
        W2=np.ascontiguousarray(np.asarray(inputs["L0_ffn_W2"], np.float32)),
        Wr=np.ascontiguousarray(np.asarray(inputs["L0_ffn_Wr"], np.float32)),
        WQT=np.ascontiguousarray(np.asarray(inputs["L1_attn_W_Q"], np.float32).T),
        WKT=np.ascontiguousarray(np.asarray(inputs["L1_attn_W_K"], np.float32).T),
        WVT=np.ascontiguousarray(np.asarray(inputs["L1_attn_W_V"], np.float32).T),
        WGT=np.ascontiguousarray(np.asarray(inputs["L1_ffn_w_gate"]).T.astype(np.int32)),
        WUT=np.ascontiguousarray(np.asarray(inputs["L1_ffn_w_up"]).T.astype(np.int32)),
        WDT=np.ascontiguousarray(np.asarray(inputs["L1_ffn_w_down"]).T.astype(np.int32)),
        ident=ident,
        vblkmask=vblkmask, diagmask=diagmask,
    )
    lampack = np.zeros((128, 2), np.float32)
    lampack[0:64, 0] = np.asarray(inputs["L0_attn_lq1"], np.float32)
    lampack[64:128, 0] = np.asarray(inputs["L0_attn_lq2"], np.float32)
    lampack[0:64, 1] = np.asarray(inputs["L0_attn_lk1"], np.float32)
    lampack[64:128, 1] = np.asarray(inputs["L0_attn_lk2"], np.float32)
    shared["identf"] = np.ascontiguousarray(
        np.concatenate([np.eye(128, dtype=np.float32), lampack], axis=1))
    in_maps = []
    for c in range(8):
        b, sub = c // 2, c % 2
        own = own_global_tiles(sub)
        oth = own_global_tiles(1 - sub)
        m = dict(shared)
        m["x_own"] = np.ascontiguousarray(
            np.concatenate([x[b, t * 128:(t + 1) * 128] for t in own], axis=0))
        m["x_other"] = np.ascontiguousarray(
            np.concatenate([x[b, t * 128:(t + 1) * 128] for t in oth], axis=0))
        m["masks"] = np.ascontiguousarray(build_masks(sub))
        in_maps.append(m)
    return in_maps


def host_gather(results):
    out = np.zeros((4, S, D), np.float32)
    for c in range(8):
        b, sub = c // 2, c % 2
        oo = results[c]["out_own"]
        for j, t in enumerate(own_global_tiles(sub)):
            out[b, t * 128:(t + 1) * 128] = oo[j * 128:(j + 1) * 128]
    return out



ENGS = ("pe", "act", "dve", "pool", "sp")


class H:
    __slots__ = ("kind", "n")

    def __init__(self, kind, n):
        self.kind = kind
        self.n = n


class Prog:

    def __init__(self, nc):
        self.nc = nc
        self.ops = {e: [] for e in ENGS}
        self.nbatch = {e: 0 for e in ENGS}
        self.ndma = {}
        self.waited = {e: {} for e in ENGS}

    def batch(self, eng, fns, waits=()):
        w = self.waited[eng]
        for h in waits:
            if h is None:
                continue
            val = h.n * (16 if h.kind.startswith("dma:") else 1)
            if w.get(h.kind, 0) >= val:
                continue
            w[h.kind] = val
            self.ops[eng].append(("wait", h.kind, val))
        for fn in fns[:-1]:
            self.ops[eng].append(("call", fn))
        self.ops[eng].append(("inc", fns[-1]))
        self.nbatch[eng] += 1
        return H(eng, self.nbatch[eng])

    def chain(self, eng, fns, waits=()):
        h = None
        for i, fn in enumerate(fns):
            h = self.batch(eng, [fn], waits=(list(waits) + [h]) if i == 0 else [h])
        return h

    def dma(self, sem, fns, waits=(), eng="sp"):
        w = self.waited[eng]
        for h in waits:
            if h is None:
                continue
            val = h.n * (16 if h.kind.startswith("dma:") else 1)
            if w.get(h.kind, 0) >= val:
                continue
            w[h.kind] = val
            self.ops[eng].append(("wait", h.kind, val))
        for fn in fns:
            self.ops[eng].append(("dma", sem, fn))
            self.ndma[sem] = self.ndma.get(sem, 0) + 1
        return H("dma:" + sem, self.ndma[sem])

    def final_wait(self):
        for sem, n in self.ndma.items():
            self.ops["sp"].append(("wait", "dma:" + sem, 16 * n))

    def realize(self, block, sems):
        def body(name):
            def run(eng):
                for item in self.ops[name]:
                    tag = item[0]
                    if tag == "wait":
                        eng.wait_ge(sems[item[1]], item[2])
                    elif tag == "call":
                        item[1](eng)
                    elif tag == "inc":
                        item[1](eng).then_inc(sems[name], 1)
                    elif tag == "dma":
                        item[2](eng).then_inc(sems["dma:" + item[1]], 16)
            return run
        block.tensor(body("pe"))
        block.scalar(body("act"))
        block.vector(body("dve"))
        block.gpsimd(body("pool"))
        block.sync(body("sp"))


class SbufMap:
    def __init__(self, nc):
        self.nc = nc

    def at(self, name, shape, dtype, off):
        per_part = int(np.prod(shape[1:])) * mybir.dt.size(dtype)
        assert off + per_part <= PBYTES, f"{name}: {off}+{per_part} > {PBYTES}"
        assert off % 32 == 0, name
        return self.nc.alloc_sbuf_tensor_at(name, shape, dtype, offset=SB_BASE + off)


OFF_XTM = 0
OFF_CONST = 65536
OFF_SMALL = 86016
OFF_PH = 88064
OFF_PH2 = 120832



class Builder:
    def __init__(self, phases, debug_outputs=()):
        self.nc = bass.Bass("TRN2", target_bir_lowering=False)
        self.phases = list(phases)
        self.debug_outputs = set(debug_outputs)
        self.p = Prog(self.nc)
        self.sb = SbufMap(self.nc)
        self.dram = {}
        self.kinds = {}
        self._declare_dram()
        self._alloc_static()
        self.h_xtm = None
        self.h_xtm_dma = None
        self.h_router_done = {}
        self.h_eproj_last = {}
        self.h_sm = {}
        self.h_mblk = {}
        self.h_mmb = {}
        self.h_gate_evac = {}
        self.h_sq = {}
        self.h_a_lastread = None

    def _dram(self, name, shape, dtype, kind):
        t = self.nc.dram_tensor(name, list(shape), dtype, kind=kind)
        self.dram[name] = t
        self.kinds[name] = kind
        return t

    def _declare_dram(self):
        E, O, I = "ExternalInput", "ExternalOutput", "Internal"
        ph = self.phases
        self._dram("ident", (128, 128), BF16, E)
        self._dram("identf", (128, 130), F32, E)
        if "A" in ph or ("B" in ph and "A" not in ph):
            self._dram("x_own", (2048, D), F32, E)
        if "A" in ph:
            self._dram("x_other", (2048, D), F32, E)
        if "B" in ph:
            for n, sh in (("Ws1", (D, 512)), ("Ws2", (512, D)), ("W1", (8, D, 512)),
                          ("W2", (8, 512, D)), ("Wr", (D, 8))):
                self._dram(n, sh, F32, E)
        if "C" in ph:
            for n in ("Wq", "Wk", "Wv"):
                self._dram(n, (D, 2048), F32, E)
            self._dram("masks", (NGRP, 4, 128, 256), BF16, E)
        if "W" in ph:
            self._dram("Wo", (D, D), F32, E)
        if "E" in ph:
            for n in ("WQT", "WKT", "WVT"):
                self._dram(n, (D, D), F32, E)
            self._dram("vblkmask", (128, 512), BF16, E)
            self._dram("diagmask", (128, 128), BF16, E)
        if "F" in ph:
            for n in ("WGT", "WUT"):
                self._dram(n, (D, 3072), I32, E)
            self._dram("WDT", (3072, D), I32, E)

        def hand(name, shape, dtype, prod, users):
            produced = prod in ph
            used = any(u in ph for u in users)
            if name in self.debug_outputs:
                kind = O
            elif produced and used:
                kind = I
            elif produced:
                kind = O
            elif used:
                kind = E
            else:
                return
            self._dram(name, shape, dtype, kind)

        hand("xn_fm_d", (D, S), BF16, "A", ("B", "C"))
        hand("x_spill_d", (2048, D), F32, "B", ("W",))
        if "C" in ph and "W" not in ph:
            self._dram("o_tm_d", (2048, D), BF16, O)
        elif "W" in ph and "C" not in ph:
            self._dram("o_tm_d", (2048, D), BF16, E)
        hand("x_mid_d", (2048, D), F32, "W", ("E",))
        hand("xn1_fm_d", (D, 2048), BF16, "D", ("E", "F"))
        if "E" in ph:
            for n in ("sq_d", "sk_d", "vv_d", "a1_d"):
                self._dram(n, (2048, D), BF16, I)
            hand("x_l1_d", (2048, D), F32, "E", ("F",))
        elif "F" in ph:
            hand("x_l1_d", (2048, D), F32, "E", ("F",))
        if "G" in ph:
            self._dram("out_own", (2048, D), F32, O)

    def _alloc_static(self):
        sb = self.sb
        self.x_tm = sb.at("x_tm", [128, NOWN * D], F32, OFF_XTM)
        self.xn_full = sb.at("xn_full", [128, 32 * 1024], BF16, OFF_XTM)
        o = OFF_CONST
        self.ident_sb = sb.at("ident_sb", [128, 128], BF16, o); o += 256
        self.identf_sb = sb.at("identf_sb", [128, 130], F32, o); o += 544
        self.ones_bf = sb.at("ones_bf", [128, 1], BF16, o); o += 32
        self.masks_sb = sb.at("masks_sb", [128, NGRP * 4 * 256], BF16, o); o += NGRP * 4 * 512
        self.vblkmask_sb = sb.at("vblkmask_sb", [128, 512], BF16, o); o += 1024
        self.diagmask_sb = sb.at("diagmask_sb", [128, 128], BF16, o); o += 256
        self.neglam_bc = sb.at("neglam_bc", [128, 1], F32, o); o += 32
        self.lam_sc = sb.at("lam_sc", [1, 4], F32, o); o += 32
        self.prob_sb = sb.at("prob_sb", [128, NOWN * 8], F32, o); o += NOWN * 32
        assert o <= OFF_SMALL, o
        o = OFF_SMALL
        self.eps_sb = sb.at("eps_sb", [128, 1], F32, o); o += 32
        self.st_a = sb.at("st_a", [128, 16], F32, o); o += 64
        self.st_b = sb.at("st_b", [128, 16], F32, o); o += 64
        self.st_c = sb.at("st_c", [128, 16], F32, o); o += 64
        self.P01 = self.nc.alloc_psum_tensor("P01", [128, 1024], F32)
        self.P23 = self.nc.alloc_psum_tensor("P23", [128, 1024], F32)
        self.Pbig = [self.P01, self.P23]
        self.P = [self.P01[:, 0:512], self.P01[:, 512:1024],
                  self.P23[:, 0:512], self.P23[:, 512:1024]]
        self.P += [self.nc.alloc_psum_tensor(f"P{i}", [128, 512], F32) for i in range(4, 8)]
        self.h_pbank = {}

    def _lowp(self, fn):
        with self.nc.allow_low_precision(reason="single-nonzero-term reduce"):
            return fn()

    def pbf16(self, i, rows=128, cols=128, row0=0, col0=0):
        return self.P[i][:, :].bitcast(BF16)[row0:row0 + rows, col0:col0 + cols]

    def load_consts(self):
        p, d = self.p, self.dram
        fns = [lambda sp: sp.dma_start(out=self.ident_sb[:, :], in_=d["ident"][:, :]),
               lambda sp: sp.dma_start(out=self.identf_sb[:, :], in_=d["identf"][:, :])]
        if "masks" in d:
            fns.append(lambda sp: sp.dma_start(
                out=bass.AP(self.masks_sb, 0, [[8192, 128], [1024, 8], [256, 4], [1, 256]]),
                in_=d["masks"][:, :, :, :].rearrange("g k p q -> p g k q")))
        if "vblkmask" in d:
            fns.append(lambda sp: sp.dma_start(out=self.vblkmask_sb[:, :], in_=d["vblkmask"][:, :]))
            fns.append(lambda sp: sp.dma_start(out=self.diagmask_sb[:, :], in_=d["diagmask"][:, :]))
        self.h_consts = p.dma("init", fns)
        self.h_init = self.h_consts
        self.h_pe_warm = p.batch("pe", [lambda t: t.matmul(
            out=self.P[7][:, 0:128], lhsT=self.ident_sb[:, :], rhs=self.ident_sb[:, :],
            start=True, stop=True)], waits=[self.h_consts])
        self.h_ones = p.batch("dve", [
            lambda v: v.memset(self.ones_bf[:, :], 1.0),
            lambda v: v.memset(self.eps_sb[:, :], 1e-5),
        ])
        self.h_ones_eps = self.h_ones
        self.h_prew1 = None
        if "A" in self.phases and "B" in self.phases:
            self.prew1 = self.sb.at("preW1", [128, 4096], F32, OFF_PH2 + 49536)
            pfns = []
            for q4 in range(2):
                srcw = bass.AP(d["W1"], q4 * 4 * 65536, [[512, 128], [65536, 4], [1, 512]])
                dstw = bass.AP(self.prew1, q4 * 2048, [[4096, 128], [512, 4], [1, 512]])
                pfns.append(lambda sp, sw=srcw, dw=dstw: sp.dma_start(out=dw, in_=sw))
            self.h_prew1 = p.dma("pw", pfns)

    def load_x_tm(self, src_name):
        p, d = self.p, self.dram
        src = d[src_name][:, :].rearrange("(j p) d -> p j d", j=16, p=128)
        dst = bass.AP(self.x_tm, 0, [[NOWN * D, 128], [D, 16], [1, D]])
        h = p.dma("xload", [lambda sp: sp.dma_start(out=dst, in_=src)])
        self.h_xtm_dma = h
        self.h_xtm = h
        return h

    def phase_a(self):
        nc, p, sb, d = self.nc, self.p, self.sb, self.dram
        self.xn_own_fm = sb.at("xn_own_fm", [128, 8 * 2048], BF16, OFF_PH)
        o = OFF_PH2
        xa = sb.at("A_xa", [128, 4 * D], F32, o); o += 4 * 4096
        xc = sb.at("A_xc", [128, 2 * D], F32, o); o += 2 * 4096
        sqd = sb.at("A_sqd", [128, 2 * D], BF16, o); o += 2 * 2048
        xnt = sb.at("A_xnt", [128, 4 * D], BF16, o); o += 4 * 2048
        xst = sb.at("A_xst", [128, 2 * D], BF16, o); o += 2 * 2048
        assert o <= PBYTES

        h_load = self.load_x_tm("x_own")
        st_a, st_b, st_c = self.st_a, self.st_b, self.st_c
        h_stats = {}
        h_sqd = {}
        h_act1 = {}
        h_xn = {}
        h_pe = {}
        h_evac = {}
        h_wr = {}
        h_oth = {}
        for s in range(32):
            par = s % 4
            u = s // 2
            up = u % 4
            if s % 2 == 1:
                waits = [self.h_init]
                if s >= 8:
                    waits.append(h_xn.get(s - 8))
                    waits.append(h_sqd.get(s - 8))
                src_d = d["x_other"][:, :].rearrange("(j p) d -> p j d", j=16, p=128)[:, u]
                h_oth[s] = p.dma(
                    f"xa{up}",
                    [lambda sp, sd=src_d, pp=up: sp.dma_start(
                        out=xa[:, pp * D:(pp + 1) * D], in_=sd)],
                    waits=waits)
                src = xa[:, up * D:(up + 1) * D]
                ld_wait = h_oth[s]
            else:
                src = self.x_tm[:, u * D:(u + 1) * D]
                ld_wait = h_load
            sq2 = s % 2
            h_sqd[s] = p.batch("act", [
                lambda a, srcc=src, pp=par, q2=sq2: a.activation(
                    out=sqd[:, q2 * D:(q2 + 1) * D], in_=srcc, func=AF.Square,
                    accum_out=st_b[:, pp:pp + 1]),
            ], waits=[ld_wait, h_xn.get(s - 4), self.h_ones, h_sqd.get(s - 2)])
            h_stats[s] = p.chain("dve", [
                lambda v, srcc=src, pp=par: v.reduce_sum(
                    out=st_a[:, 8 + pp:9 + pp], in_=srcc, axis=mybir.AxisListType.X),
                lambda v, pp=par: v.tensor_scalar_mul(
                    out=st_a[:, pp:pp + 1], in0=st_a[:, 8 + pp:9 + pp], scalar1=1.0 / D),
                lambda v, pp=par: v.scalar_tensor_tensor(
                    out=st_a[:, 4 + pp:5 + pp], in0=st_a[:, pp:pp + 1],
                    scalar=-1.0, in1=st_a[:, pp:pp + 1], op0=ALU.mult, op1=ALU.mult),
                lambda v, pp=par: v.tensor_scalar_add(
                    out=st_a[:, 12 + pp:13 + pp], in0=st_a[:, 4 + pp:5 + pp], scalar1=1e-5),
            ], waits=[ld_wait, h_xn.get(s - 4)])
            h_act1[s] = p.batch("act", [
                lambda a, pp=par: a.activation(
                    out=st_b[:, 8 + pp:9 + pp], in_=st_b[:, pp:pp + 1], func=AF.Sqrt,
                    scale=1.0 / D, bias=st_a[:, 12 + pp:13 + pp]),
            ], waits=[h_stats[s], h_sqd[s]])
            h_xn[s] = p.chain("dve", [
                lambda v, pp=par: v.reciprocal(
                    out=st_c[:, pp:pp + 1], in_=st_b[:, 8 + pp:9 + pp]),
                lambda v, pp=par: v.scalar_tensor_tensor(
                    out=st_c[:, 4 + pp:5 + pp], in0=st_a[:, pp:pp + 1],
                    scalar=-1.0, in1=st_c[:, pp:pp + 1], op0=ALU.mult, op1=ALU.mult),
                lambda v, srcc=src, pp=par: v.tensor_scalar(
                    out=xnt[:, pp * D:(pp + 1) * D], in0=srcc,
                    scalar1=st_c[:, pp:pp + 1], scalar2=st_c[:, 4 + pp:5 + pp],
                    op0=ALU.mult, op1=ALU.add),
            ], waits=[h_act1[s]])
            for ch in range(8):
                bank = ch % 4
                waits = [h_xn[s], self.h_init, self.h_pbank.get(bank)]
                h_pe[(s, ch)] = p.batch("pe", [
                    lambda t, pp=par, cc=ch, bb=bank: t.transpose(
                        out=self.pbf16(bb),
                        in_=xnt[:, pp * D + cc * 128: pp * D + (cc + 1) * 128],
                        identity=self.ident_sb[:, :]),
                ], waits=waits)
                if s % 2 == 0:
                    dst = self.xn_own_fm[:, ch * 2048 + u * 128: ch * 2048 + (u + 1) * 128]
                else:
                    dst = xst[:, (u % 2) * D + ch * 128: (u % 2) * D + (ch + 1) * 128]
                h_evac[(s, ch)] = p.batch("act", [
                    lambda a, dd2=dst, bb=bank: a.copy(out=dd2, in_=self.pbf16(bb)),
                ], waits=[h_pe[(s, ch)]] + ([h_wr.get(s - 4)] if s % 2 == 1 else []))
                self.h_pbank[bank] = h_evac[(s, ch)]
            if s % 2 == 0:
                srcsb = bass.AP(self.xn_own_fm, u * 128, [[16384, 128], [2048, 8], [1, 128]])
            else:
                srcsb = bass.AP(xst, (u % 2) * D, [[2 * D, 128], [128, 8], [1, 128]])
            dstd = bass.AP(d["xn_fm_d"], s * 128, [[S, 128], [128 * S, 8], [1, 128]])
            sem = "stO" if s % 2 == 0 else f"st{u % 2}"
            h_wr[s] = p.dma(sem, [
                lambda sp, ss=srcsb, dd2=dstd: sp.dma_start(out=dd2, in_=ss)],
                waits=[h_evac[(s, 7)]])
        self.h_xn_fm_written = None
        self.h_a_lastread = h_stats[30]
        self.h_xn_own_ready = h_evac[(30, 7)]
        self.h_a_region = [h_sqd[31], h_stats[31], h_pe[(31, 7)], h_oth[31],
                           h_wr[29], h_wr[31]]
        self.h_a_xn_stores = [h_wr[30], h_wr[29], h_wr[31]]

    def phase_b(self):
        nc, p, sb, d = self.nc, self.p, self.sb, self.dram
        if "A" not in self.phases:
            self.load_x_tm("x_own")
            self.xn_own_fm = sb.at("xn_own_fm", [128, 8 * 2048], BF16, OFF_PH)
            fns = []
            for ch in range(8):
                srcap = bass.AP(d["xn_fm_d"], ch * 128 * S, [[S, 128], [256, 16], [1, 128]])
                dstap = bass.AP(self.xn_own_fm, ch * 2048, [[16384, 128], [128, 16], [1, 128]])
                fns.append(lambda sp, ss=srcap, dd2=dstap: sp.dma_start(out=dd2, in_=ss))
            h = p.dma("xnload", fns)
            self.h_xn_own_ready = h
            h_xtm_readers = None
        else:
            h_xtm_readers = self.h_a_lastread
        o = OFF_PH2
        wrst = sb.at("B_wrst", [128, 64], F32, o); o += 256
        wrbf = sb.at("B_wrbf", [128, 64], BF16, o); o += 128
        wst = sb.at("B_wst", [128, 2 * 2048], F32, o); o += 16384
        w1bf = sb.at("B_w1bf", [128, 2 * 4096], BF16, o); o += 16384
        w2bf = sb.at("B_w2bf", [128, 2 * 4096], BF16, o); o += 16384
        he = sb.at("B_he", [128, 2 * 4 * 2048], BF16, o); o += 32768
        rtile = sb.at("B_rtile", [128, 2 * 512], BF16, o); o += 2048
        ytile = sb.at("B_ytile", [128, 2 * 512], BF16, o); o += 2048
        lg = sb.at("B_lg", [128, 8], F32, o); o += 32
        q8 = sb.at("B_q8", [128, 8], F32, o); o += 32
        rt = sb.at("B_rt", [128, 8], F32, o); o += 32
        assert o <= PBYTES, o

        xn = self.xn_own_fm
        h_xn = getattr(self, "h_xn_own_ready", None)
        bank_reader = {}
        stage_slot = {}
        h_gate_last = {}
        h_down_last = {}
        h_prob = {}

        a_reg = getattr(self, "h_a_region", None) or []

        srcwr = bass.AP(d["Wr"], 0, [[8, 128], [1024, 8], [1, 8]])
        dstwr = bass.AP(wrst, 0, [[64, 128], [8, 8], [1, 8]])
        h_wr = p.dma("wrload", [lambda sp: sp.dma_start(out=dstwr, in_=srcwr)],
                     waits=a_reg)
        h_wrbf = p.batch("pool", [lambda g: g.tensor_copy(out=wrbf[:, :], in_=wrst[:, :])],
                         waits=[h_wr])

        for j in range(16):
            bank = j % 2
            fns = []
            for ch in range(8):
                fns.append(lambda t, jj=j, cc=ch, bb=bank: t.matmul(
                    out=self.P[bb][:, 0:8],
                    lhsT=xn[:, cc * 2048 + jj * 128: cc * 2048 + (jj + 1) * 128],
                    rhs=wrbf[:, cc * 8:(cc + 1) * 8],
                    start=(cc == 0), stop=(cc == 7)))
            h_mm = p.batch("pe", fns, waits=[h_wrbf, h_xn, bank_reader.get(bank)])
            h1 = p.chain("dve", [
                lambda v, bb=bank: v.tensor_copy(out=lg[:, :], in_=self.P[bb][:, 0:8]),
                lambda v: v.reduce_max(out=rt[:, 0:1], in_=lg[:, :], axis=mybir.AxisListType.X),
                lambda v: v.tensor_scalar_mul(out=rt[:, 3:4], in0=rt[:, 0:1], scalar1=-1.0),
                lambda v: v.tensor_single_scalar(out=q8[:, :], in_=lg[:, :],
                                                 scalar=rt[:, 0:1], op=ALU.is_equal),
                lambda v: v.scalar_tensor_tensor(out=q8[:, :], in0=q8[:, :], scalar=-1e30,
                                                 in1=lg[:, :], op0=ALU.mult, op1=ALU.add),
                lambda v: v.reduce_max(out=rt[:, 1:2], in_=q8[:, :], axis=mybir.AxisListType.X),
            ], waits=[h_mm])
            bank_reader[bank] = h1
            h2 = p.batch("act", [
                lambda a: a.activation(out=rt[:, 4:5], in_=rt[:, 1:2], func=AF.Exp,
                                       bias=rt[:, 3:4], scale=1.0)], waits=[h1])
            h3 = p.batch("dve", [
                lambda v: v.tensor_scalar_add(out=rt[:, 5:6], in0=rt[:, 4:5], scalar1=1.0)],
                waits=[h2])
            h4 = p.batch("act", [
                lambda a: a.activation(out=q8[:, :], in_=lg[:, :], func=AF.Exp,
                                       bias=rt[:, 3:4], scale=1.0)], waits=[h3])
            h_prob[j] = p.batch("dve", [
                lambda v, jj=j: v.tensor_single_scalar(
                    out=self.prob_sb[:, jj * 8:(jj + 1) * 8], in_=q8[:, :],
                    scalar=rt[:, 5:6], op=ALU.is_ge)], waits=[h4])

        h_add = {}
        h_add_seq = {}
        h_y = {}
        for e in range(9):
            ep = e % 2
            if e == 0 and self.h_prew1 is not None:
                for q4 in range(2):
                    hc = p.batch("dve", [
                        lambda v, qq=q4: v.tensor_copy(
                            out=w1bf[:, qq * 2048:(qq + 1) * 2048],
                            in_=self.prew1[:, qq * 2048:(qq + 1) * 2048])],
                        waits=[self.h_prew1] + a_reg)
            else:
                for q4 in range(2):
                    sl = q4
                    base = (0 if e == 8 else e * 524288) + q4 * 4 * 65536
                    srcw = bass.AP(d["Ws1"] if e == 8 else d["W1"], base,
                                   [[512, 128], [65536, 4], [1, 512]])
                    dstw = bass.AP(wst, sl * 2048, [[4096, 128], [512, 4], [1, 512]])
                    hld = p.dma(f"ws{sl}", [lambda sp, sw=srcw, dw=dstw: sp.dma_start(
                        out=dw, in_=sw)],
                        waits=([stage_slot[sl]] if sl in stage_slot else a_reg))
                    hc = p.batch("dve", [
                        lambda v, ss=sl, pp=ep, qq=q4: v.tensor_copy(
                            out=w1bf[:, pp * 4096 + qq * 2048: pp * 4096 + (qq + 1) * 2048],
                            in_=wst[:, ss * 2048:(ss + 1) * 2048])],
                        waits=[hld, h_gate_last.get(e - 2)] + a_reg)
                    stage_slot[sl] = hc
            h_w1_ready = hc
            for q4 in range(2):
                sl = q4
                base = (0 if e == 8 else e * 524288) + q4 * 2 * 131072
                srcw = bass.AP(d["Ws2"] if e == 8 else d["W2"], base,
                               [[1024, 128], [131072, 2], [1, 1024]])
                dstw = bass.AP(wst, sl * 2048, [[4096, 128], [1024, 2], [1, 1024]])
                hld = p.dma(f"ws{sl}", [lambda sp, sw=srcw, dw=dstw: sp.dma_start(
                    out=dw, in_=sw)],
                    waits=([stage_slot[sl]] if sl in stage_slot else a_reg))
                hc = p.batch("dve", [
                    lambda v, ss=sl, pp=ep, qq=q4: v.tensor_copy(
                        out=w2bf[:, pp * 4096 + qq * 2048: pp * 4096 + (qq + 1) * 2048],
                        in_=wst[:, ss * 2048:(ss + 1) * 2048])],
                    waits=[hld, h_down_last.get(e - 2)] + a_reg)
                stage_slot[sl] = hc
            h_w2_ready = hc
            for tg in range(4):
                for fc in range(4):
                    bank = 2 + (fc % 2)
                    fns = []
                    for ch in range(8):
                        fns.append(lambda t, cc=ch, ff=fc, gg=tg, pp=ep, bb=bank: t.matmul(
                            out=self.P[bb][:, :],
                            lhsT=w1bf[:, pp * 4096 + cc * 512 + ff * 128:
                                      pp * 4096 + cc * 512 + (ff + 1) * 128],
                            rhs=xn[:, cc * 2048 + gg * 512: cc * 2048 + (gg + 1) * 512],
                            start=(cc == 0), stop=(cc == 7)))
                    hg = p.batch("pe", fns,
                                 waits=[h_w1_ready, h_xn, bank_reader.get(bank)])
                    if tg == 3 and fc == 3:
                        h_gate_last[e] = hg
                    hr = p.batch("act", [lambda a, bb=bank, ff=fc: a.activation(
                        out=rtile[:, (ff % 2) * 512:((ff % 2) + 1) * 512],
                        in_=self.P[bb][:, :], func=AF.Relu)],
                        waits=[hg, self.h_sq.get((e, tg, fc - 2)),
                               self.h_sq.get((e, tg - 1, fc + 2))])
                    bank_reader[bank] = hr
                    self.h_sq[(e, tg, fc)] = p.batch("dve", [
                        lambda v, ff=fc, gg=tg, pp=ep: v.tensor_tensor(
                            out=he[:, pp * 8192 + ff * 2048 + gg * 512:
                                    pp * 8192 + ff * 2048 + (gg + 1) * 512],
                            in0=rtile[:, (ff % 2) * 512:((ff % 2) + 1) * 512],
                            in1=rtile[:, (ff % 2) * 512:((ff % 2) + 1) * 512],
                            op=ALU.mult)],
                        waits=[hr, h_down_last.get(e - 2)])
                for jj in range(4):
                    j = tg * 4 + jj
                    for mo in range(2):
                        bank = 4 + ((j * 2 + mo) % 2)
                        fns = []
                        for fc in range(4):
                            fns.append(lambda t, cc=fc, jo=j, mm=mo, pp=ep, bb=bank: t.matmul(
                                out=self.P[bb][:, :],
                                lhsT=he[:, pp * 8192 + cc * 2048 + jo * 128:
                                         pp * 8192 + cc * 2048 + (jo + 1) * 128],
                                rhs=w2bf[:, pp * 4096 + cc * 1024 + mm * 512:
                                         pp * 4096 + cc * 1024 + (mm + 1) * 512],
                                start=(cc == 0), stop=(cc == 3)))
                        hdp = p.batch("pe", fns, waits=[
                            h_w2_ready,
                            self.h_sq[(e, tg, 0)], self.h_sq[(e, tg, 1)],
                            self.h_sq[(e, tg, 2)], self.h_sq[(e, tg, 3)],
                            bank_reader.get(bank)])
                        if j == 15 and mo == 1:
                            h_down_last[e] = hdp
                        yslot = (j * 2 + mo) % 2
                        gstep = e * 32 + j * 2 + mo
                        if e == 8:
                            h_add[(e, j, mo)] = p.batch("dve", [
                                lambda v, bb=bank, jo=j, mm=mo: v.tensor_tensor(
                                    out=self.x_tm[:, jo * 1024 + mm * 512:
                                                  jo * 1024 + (mm + 1) * 512],
                                    in0=self.x_tm[:, jo * 1024 + mm * 512:
                                                  jo * 1024 + (mm + 1) * 512],
                                    in1=self.P[bb][:, :], op=ALU.add)],
                                waits=[hdp, h_add.get((e - 1, j, mo)), h_xtm_readers,
                                       self.h_xtm_dma])
                        else:
                            h_add[(e, j, mo)] = p.batch("dve", [
                                lambda v, bb=bank, jo=j, mm=mo, ee=e: v.scalar_tensor_tensor(
                                    out=self.x_tm[:, jo * 1024 + mm * 512:
                                                  jo * 1024 + (mm + 1) * 512],
                                    in0=self.P[bb][:, :],
                                    scalar=self.prob_sb[:, jo * 8 + ee: jo * 8 + ee + 1],
                                    in1=self.x_tm[:, jo * 1024 + mm * 512:
                                                  jo * 1024 + (mm + 1) * 512],
                                    op0=ALU.mult, op1=ALU.add)],
                                waits=[hdp, h_prob[j], h_add.get((e - 1, j, mo)),
                                       h_xtm_readers, self.h_xtm_dma])
                        bank_reader[bank] = h_add[(e, j, mo)]
                        h_add_seq[gstep] = h_add[(e, j, mo)]
                        self.h_xtm = h_add[(e, j, mo)]
        self.h_xtm = p.batch("pool", [lambda g: g.engine_nop()],
                             waits=[h_add[(8, 15, 0)], h_add[(8, 15, 1)]])

        srcx = bass.AP(self.x_tm, 0, [[NOWN * D, 128], [D, 16], [1, D]])
        dstx = d["x_spill_d"][:, :].rearrange("(j p) d -> p j d", j=16, p=128)
        h_all_adds = [h_add[(8, j, mo)] for j in range(16) for mo in range(2)]
        self.h_spill = p.dma("spill", [lambda sp: sp.dma_start(out=dstx, in_=srcx)],
                             waits=h_all_adds)

    def phase_c(self):
        nc, p, sb, d = self.nc, self.p, self.sb, self.dram
        h_sp = getattr(self, "h_spill", None)
        fns = []
        for ch in range(8):
            src = d["xn_fm_d"][ch * 128:(ch + 1) * 128, :]
            fns.append(lambda sp, ss=src, cc=ch: sp.dma_start(
                out=self.xn_full[:, cc * S:(cc + 1) * S], in_=ss))
        h_xnf = p.dma("xnfull", fns,
                      waits=[h_sp] + (getattr(self, "h_a_xn_stores", None) or []))

        o = OFF_PH
        o_tm = sb.at("C_otm", [128, 16 * 1024], BF16, o); o += 32768
        self.o_tm = o_tm
        o = OFF_PH2
        wst = sb.at("C_wst", [128, 2 * 256], F32, o); o += 2048
        wq_bf = sb.at("C_wq", [128, 2 * 2048], BF16, o); o += 8192
        wk_bf = sb.at("C_wk", [128, 2 * 2048], BF16, o); o += 8192
        wv_bf = sb.at("C_wv", [128, 2 * 2048], BF16, o); o += 8192
        q_bf = sb.at("C_q", [128, 2 * 2048], BF16, o); o += 8192
        k_bf = sb.at("C_k", [128, 2 * 4096], BF16, o); o += 16384
        vT = sb.at("C_vT", [128, 32 * 256], BF16, o); o += 16384
        e_sb = sb.at("C_e", [128, 2 * 4 * 256], BF16, o); o += 4096
        av_sb = sb.at("C_av", [128, 2 * 256], BF16, o); o += 1024
        d1_sb = sb.at("C_d1", [64, 512], F32, o); o += 2048
        rd_sb = sb.at("C_rd", [128, 8], F32, o); o += 32
        tmp_sb = sb.at("C_tmp", [128, 64], F32, o); o += 256
        lvec = sb.at("C_lvec", [1, 256], F32, o); o += 1024
        lsc = sb.at("C_lsc", [1, 8], F32, o); o += 32
        onesrow = sb.at("C_onesrow", [1, 128], F32, o); o += 512
        assert o <= PBYTES, o

        h_ones = p.batch("dve", [
            lambda v: v.memset(onesrow[:, :], 1.0),
            lambda v: v.memset(lvec[:, 0:1], 1.0),
        ])
        onesf = lvec
        h_onesf = p.batch("dve", [lambda v: v.memset(self.st_c[:, 4:5], 1.0)])
        h1 = p.batch("dve", [
            lambda v: v.tensor_tensor(out=self.st_c[:, 5:6], in0=self.identf_sb[:, 128:129],
                                      in1=self.identf_sb[:, 129:130], op=ALU.mult),
        ], waits=[self.h_init])
        h4 = p.batch("pe", [
            lambda t: t.matmul(out=self.P[6][0:1, 0:1], lhsT=self.st_c[0:64, 5:6],
                               rhs=self.st_c[0:64, 4:5], start=True, stop=True,
                               skip_group_check=True),
            lambda t: t.matmul(out=self.P[6][0:1, 1:2], lhsT=self.st_c[64:128, 5:6],
                               rhs=self.st_c[64:128, 4:5], start=False, stop=True,
                               tile_position=(64, 0), skip_group_check=True),
        ], waits=[h1, h_onesf, self.h_pe_warm])
        h5 = p.batch("dve", [lambda v: v.tensor_copy(out=lsc[:, 0:2],
                                                     in_=self.P[6][0:1, 0:2])], waits=[h4])
        h6 = p.batch("act", [
            lambda a: a.activation(out=lsc[:, 2:4], in_=lsc[:, 0:2], func=AF.Exp),
        ], waits=[h5])
        h7 = p.chain("dve", [
            lambda v: v.tensor_tensor(out=lsc[:, 4:5], in0=lsc[:, 3:4],
                                      in1=lsc[:, 2:3], op=ALU.subtract),
            lambda v: v.tensor_scalar_add(out=lsc[:, 5:6], in0=lsc[:, 4:5], scalar1=-0.8),
        ], waits=[h6])
        h8 = p.batch("pe", [lambda t: t.matmul(
            out=self.P[6][0:128, 4:5], lhsT=onesrow[:, :], rhs=lsc[:, 5:6],
            start=True, stop=True, skip_group_check=True)], waits=[h7, h_ones, h5])
        h_neglam = p.batch("dve", [lambda v: v.tensor_copy(
            out=self.neglam_bc[:, :], in_=self.P[6][0:128, 4:5])], waits=[h8])
        h_p6_free = h_neglam

        bank_reader = {6: h_p6_free, 7: self.h_pe_warm}
        stage_slot = {}
        if "B" in self.phases:
            stage_slot[0] = self.h_xtm
            stage_slot[1] = self.h_xtm
        h_proj_last = {}
        h_wcast = {}
        h_attn_pe_last = {}
        masks = self.masks_sb

        def wload(hp, wname, ch, sl):
            src = bass.AP(d[wname], ch * 128 * 2048 + hp * 128,
                          [[2048, 128], [1024, 2], [1, 128]])
            return p.dma(f"cw{sl}", [lambda sp, ss=src, s2=sl: sp.dma_start(
                out=wst[:, s2 * 256:(s2 + 1) * 256], in_=ss)],
                waits=[stage_slot.get(sl)])

        for hp in range(C_NHP):
            par = hp % 2
            for wi, (wname, wbf) in enumerate((("Wq", wq_bf), ("Wk", wk_bf), ("Wv", wv_bf))):
                for ch in range(8):
                    sl = (wi * 8 + ch) % 2
                    hld = wload(hp, wname, ch, sl)
                    hc = p.batch("pool", [
                        lambda g, ss=sl, pp=par, cc=ch, wb=wbf: g.tensor_copy(
                            out=wb[:, pp * 2048 + cc * 256:(pp * 2048 + (cc + 1) * 256)],
                            in_=wst[:, ss * 256:(ss + 1) * 256])],
                        waits=[hld, h_proj_last.get(hp - 2)])
                    stage_slot[sl] = hc
            h_w_ready = hc

            h_pp = {}
            pj = 0
            for c in range(2):
                for blk in range(4):
                    bank = pj % 2
                    pj += 1
                    rhs = bass.AP(self.xn_full, 0, [[32768, 128], [0, 0], [0, 0]])
                    fns = []
                    for ch in range(8):
                        rr = bass.AP(self.xn_full, ch * S + blk * 1024,
                                     [[32768, 128], [256, 4], [1, 128]])
                        fns.append(lambda t, cc=ch, cp=c, pp=par, bb=bank, r2=rr: t.matmul(
                            out=self.P[bb][:, :],
                            lhsT=wq_bf[:, pp * 2048 + cc * 256 + cp * 128:
                                       pp * 2048 + cc * 256 + (cp + 1) * 128],
                            rhs=r2, start=(cc == 0), stop=(cc == 7)))
                    hm = p.batch("pe", fns, waits=[h_w_ready, h_xnf, bank_reader.get(bank)])
                    hv = p.batch("dve", [lambda v, bb=bank, cp=c, kk=blk: v.tensor_scalar_mul(
                        out=q_bf[:, cp * 2048 + kk * 512:(cp * 2048 + (kk + 1) * 512)],
                        in0=self.P[bb][:, :], scalar1=1.0)], waits=[hm])
                    bank_reader[bank] = hv
            for c in range(2):
                for blk in range(8):
                    bank = pj % 2
                    pj += 1
                    fns = []
                    for ch in range(8):
                        rr = self.xn_full[:, ch * S + blk * 512: ch * S + (blk + 1) * 512]
                        fns.append(lambda t, cc=ch, cp=c, pp=par, bb=bank, r2=rr: t.matmul(
                            out=self.P[bb][:, :],
                            lhsT=wk_bf[:, pp * 2048 + cc * 256 + cp * 128:
                                       pp * 2048 + cc * 256 + (cp + 1) * 128],
                            rhs=r2, start=(cc == 0), stop=(cc == 7)))
                    hm = p.batch("pe", fns, waits=[h_w_ready, h_xnf, bank_reader.get(bank)])
                    hv = p.batch("dve", [lambda v, bb=bank, cp=c, kk=blk: v.tensor_scalar_mul(
                        out=k_bf[:, cp * 4096 + kk * 512:(cp * 4096 + (kk + 1) * 512)],
                        in0=self.P[bb][:, :], scalar1=1.0)], waits=[hm])
                    bank_reader[bank] = hv
            for slot in range(32):
                bank = pj % 2
                pj += 1
                fns = []
                for ch in range(8):
                    fns.append(lambda t, cc=ch, ssl=slot, pp=par, bb=bank: t.matmul(
                        out=self.P[bb][:, 0:256],
                        lhsT=self.xn_full[:, cc * S + ssl * 128: cc * S + (ssl + 1) * 128],
                        rhs=wv_bf[:, pp * 2048 + cc * 256:(pp * 2048 + (cc + 1) * 256)],
                        start=(cc == 0), stop=(cc == 7)))
                hm = p.batch("pe", fns, waits=[h_w_ready, h_xnf, bank_reader.get(bank)])
                hv = p.batch("dve", [lambda v, bb=bank, ssl=slot: v.tensor_scalar_mul(
                    out=vT[:, ssl * 256:(ssl + 1) * 256],
                    in0=self.P[bb][:, 0:256], scalar1=0.2)], waits=[hm])
                bank_reader[bank] = hv
            h_proj_last[hp] = hv

            hta = h_proj_last[hp]
            self.h_otm_last = hta
            for i in range(C_NGRP if "noattn" not in C_CUT else 0):
                h_exp = {}
                h_avk = {}
                h_emask = {}
                h_eready = {}
                nk = 4 * i + 4

                def emit_av(k, p=p):
                    kp = k % 2
                    first = (k == 0)
                    afns = []
                    for c in range(2):
                        for hm in range(2):
                            afns.append(lambda t, cp=c, hh=hm, kk=k, pp2=kp, ff=first: t.matmul(
                                out=self.P[4 + cp][hh * 64:(hh + 1) * 64, 0:256],
                                lhsT=vT[:, kk * 256 + cp * 128 + hh * 64:
                                        kk * 256 + cp * 128 + (hh + 1) * 64],
                                rhs=e_sb[:, pp2 * 1024 + hh * 512 + cp * 256:
                                         pp2 * 1024 + hh * 512 + (cp + 1) * 256],
                                start=ff, stop=False,
                                tile_position=(0, hh * 64), skip_group_check=True))
                    for hm in range(2):
                        afns.append(lambda t, hh=hm, kk=k, pp2=kp, ff=first: t.matmul(
                            out=self.P[6][hh * 32:hh * 32 + 1, 0:512],
                            lhsT=self.ones_bf[:, :],
                            rhs=e_sb[:, pp2 * 1024 + hh * 512: pp2 * 1024 + (hh + 1) * 512],
                            start=ff, stop=False,
                            tile_position=(0, hh * 32), skip_group_check=True))
                    waits = [h_eready[k]]
                    if first:
                        waits += [bank_reader.get(4), bank_reader.get(5), bank_reader.get(6)]
                    return p.batch("pe", afns, waits=waits)

                for k in range(nk):
                    kp = k % 2
                    sb_bank = 2 * kp
                    fns = []
                    for c in range(2):
                        for hm in range(2):
                            fns.append(lambda t, cp=c, hh=hm, kk=k, ii=i, bb=sb_bank: t.matmul(
                                out=self.P[bb + hh][:, cp * 256:(cp + 1) * 256],
                                lhsT=k_bf[hh * 64:(hh + 1) * 64,
                                          cp * 4096 + kk * 128: cp * 4096 + (kk + 1) * 128],
                                rhs=q_bf[hh * 64:(hh + 1) * 64,
                                         cp * 2048 + ii * 256: cp * 2048 + (ii + 1) * 256],
                                start=(cp == 0), stop=True,
                                tile_position=(hh * 64, 0), skip_group_check=True))
                    if "nosc" in C_CUT:
                        fns = [lambda t: t.nop()]
                    hsc = p.batch("pe", fns, waits=[
                        h_proj_last[hp],
                        h_exp.get(k - 2), bank_reader.get(sb_bank),
                        bank_reader.get(sb_bank + 1)])
                    if k >= 1:
                        h_avk[k - 1] = emit_av(k - 1)
                    efns = [lambda a, pp2=kp: a.activation(
                        out=e_sb[:, pp2 * 1024: (pp2 + 1) * 1024],
                        in_=self.Pbig[pp2][:, :],
                        func=AF.Exp, scale=0.125)]
                    if "noexp" in C_CUT:
                        efns = [lambda a: a.nop()]
                    h_exp[k] = p.batch("act", efns, waits=[
                        hsc, h_avk.get(k - 2), h_emask.get(k - 2)])
                    bank_reader[sb_bank] = h_exp[k]
                    bank_reader[sb_bank + 1] = h_exp[k]
                    h_eready[k] = h_exp[k]
                    if k >= 4 * i and "nomask" not in C_CUT:
                        kk2 = k - 4 * i
                        mfns = []
                        for hm in range(2):
                            mfns.append(lambda g, pp2=kp, hh=hm, ii=i, k3=kk2: g.tensor_tensor(
                                out=bass.AP(e_sb, pp2 * 1024 + hh * 512,
                                            [[2048, 128], [256, 2], [1, 256]]),
                                in0=bass.AP(e_sb, pp2 * 1024 + hh * 512,
                                            [[2048, 128], [256, 2], [1, 256]]),
                                in1=bass.AP(self.masks_sb, (ii * 4 + k3) * 256,
                                            [[NGRP * 4 * 256, 128], [0, 2], [1, 256]]),
                                op=ALU.mult))
                        h_emask[k] = p.batch("dve", mfns, waits=[h_exp[k], self.h_init])
                        h_eready[k] = h_emask[k]
                h_avk[nk - 1] = emit_av(nk - 1)
                h_av = h_avk[nk - 1]
                h_av_ev = p.batch("act", [
                    lambda a: a.copy(out=av_sb[:, 0:256], in_=self.P[4][:, 0:256]),
                    lambda a: a.copy(out=av_sb[:, 256:512], in_=self.P[5][:, 0:256]),
                ], waits=[h_av])
                h_d_ev = p.batch("dve", [
                    lambda v: v.tensor_copy(out=d1_sb[0:1, :], in_=self.P[6][0:1, 0:512]),
                    lambda v: v.tensor_copy(out=d1_sb[32:33, :], in_=self.P[6][32:33, 0:512]),
                ], waits=[h_av])
                bank_reader[4] = h_av_ev
                bank_reader[5] = h_av_ev
                bank_reader[6] = h_d_ev
                for q2 in range(2):
                    j = 2 * i + q2
                    htd = p.batch("pe", [
                        lambda t, qq=q2: t.transpose(
                            out=self.P[0][:, 0:64],
                            in_=d1_sb[:, qq * 128:(qq + 1) * 128],
                            identity=self.identf_sb[0:64, 0:64]),
                        lambda t, qq=q2: t.transpose(
                            out=self.P[1][:, 0:64],
                            in_=d1_sb[:, 256 + qq * 128: 256 + (qq + 1) * 128],
                            identity=self.identf_sb[0:64, 0:64]),
                    ], waits=[h_d_ev, bank_reader.get(0), bank_reader.get(1)])
                    hrd = p.chain("dve", [
                        lambda v: v.reciprocal(
                            out=rd_sb[:, 0:2],
                            in_=bass.AP(self.P01, 0, [[1024, 128], [32, 2]])),
                        lambda v: v.reciprocal(
                            out=rd_sb[:, 2:4],
                            in_=bass.AP(self.P01, 512, [[1024, 128], [32, 2]])),
                        lambda v: v.tensor_scalar_mul(
                            out=rd_sb[:, 4:6], in0=rd_sb[:, 2:4],
                            scalar1=self.neglam_bc[:, :]),
                    ], waits=[htd, h_neglam])
                    bank_reader[0] = hrd
                    bank_reader[1] = hrd
                    hta = p.batch("pe", [
                        lambda t, qq=q2: t.transpose(
                            out=self.pbf16(2), in_=av_sb[:, qq * 128:(qq + 1) * 128],
                            identity=self.ident_sb[:, :]),
                        lambda t, qq=q2: t.transpose(
                            out=self.pbf16(3), in_=av_sb[:, 256 + qq * 128: 256 + (qq + 1) * 128],
                            identity=self.ident_sb[:, :]),
                    ], waits=[h_av_ev, bank_reader.get(2), bank_reader.get(3), self.h_init])
                    hdiv = None
                    for hm in range(2):
                        hdiv = p.chain("dve", [
                            lambda v, hh=hm: v.tensor_scalar_mul(
                                out=tmp_sb[:, :],
                                in0=self.pbf16(2)[:, hh * 64:(hh + 1) * 64],
                                scalar1=rd_sb[:, hh:hh + 1]),
                            lambda v, hh=hm, jj=j, hp2=hp: v.scalar_tensor_tensor(
                                out=o_tm[:, jj * 1024 + hp2 * 128 + hh * 64:
                                         jj * 1024 + hp2 * 128 + (hh + 1) * 64],
                                in0=self.pbf16(3)[:, hh * 64:(hh + 1) * 64],
                                scalar=rd_sb[:, 4 + hh:5 + hh],
                                in1=tmp_sb[:, :], op0=ALU.mult, op1=ALU.add),
                        ], waits=[hta, hrd, hdiv])
                    bank_reader[2] = hdiv
                    bank_reader[3] = hdiv
                    self.h_otm_last = hdiv
            h_attn_pe_last[hp] = h_av if False else hta
        self.h_c_done = [self.h_otm_last, h_attn_pe_last[C_NHP - 1]]
        if "W" not in self.phases:
            srco = bass.AP(o_tm, 0, [[16384, 128], [1024, 16], [1, 1024]])
            dsto = d["o_tm_d"][:, :].rearrange("(j p) e -> p j e", j=16, p=128)
            p.dma("osave", [lambda sp: sp.dma_start(out=dsto, in_=srco)],
                  waits=[self.h_otm_last])

    def phase_w(self):
        nc, p, sb, d = self.nc, self.p, self.sb, self.dram
        o = OFF_PH2
        o_fm = sb.at("W_ofm", [128, 8 * 2048], BF16, o); o += 32768
        wst = sb.at("W_wst", [128, 2 * 512], F32, o); o += 4096
        wo_bf = sb.at("W_wo", [128, 8 * 1024], BF16, o); o += 16384
        at_sb = sb.at("W_at", [128, 2 * 512], BF16, o); o += 2048
        assert o <= PBYTES, o

        h_cdone = getattr(self, "h_c_done", [])
        if "C" in self.phases:
            o_tm = self.o_tm
            h_osrc = self.h_otm_last
        else:
            o_tm = sb.at("C_otm", [128, 16 * 1024], BF16, OFF_PH)
            srco = d["o_tm_d"][:, :].rearrange("(j p) e -> p j e", j=16, p=128)
            dsto = bass.AP(o_tm, 0, [[16384, 128], [1024, 16], [1, 1024]])
            h_osrc = p.dma("oload", [lambda sp: sp.dma_start(out=dsto, in_=srco)])

        bank_reader = {}
        stage_slot = {}
        if "C" in self.phases:
            for _sl in (0, 1):
                stage_slot[_sl] = self.h_otm_last
        h_wc0 = {}
        for ch in range(8):
            for half in range(2):
                sl = half
                src = d["Wo"][:, :].rearrange("(c m) o -> m c o", c=8, m=128)[
                    :, ch, half * 512:(half + 1) * 512]
                hld = p.dma(f"wo{sl}", [lambda sp, ss=src, s2=sl: sp.dma_start(
                    out=wst[:, s2 * 512:(s2 + 1) * 512], in_=ss)],
                    waits=[stage_slot.get(sl)])
                hc = p.batch("dve", [lambda v, s2=sl, cc=ch, hh=half: v.tensor_copy(
                    out=wo_bf[:, cc * 1024 + hh * 512: cc * 1024 + (hh + 1) * 512],
                    in_=wst[:, s2 * 512:(s2 + 1) * 512])], waits=[hld])
                stage_slot[sl] = hc
        h_wo_ready = hc
        srcx = d["x_spill_d"][:, :].rearrange("(j p) d -> p j d", j=16, p=128)
        dstx = bass.AP(self.x_tm, 0, [[NOWN * D, 128], [D, 16], [1, D]])
        h_xl = p.dma("xload2", [lambda sp: sp.dma_start(out=dstx, in_=srcx)],
                     waits=h_cdone)
        self.h_xtm_dma = h_xl
        self.h_xtm = h_xl
        h_ofm = {}
        for j in range(16):
            for ch in range(8):
                bank = (j * 8 + ch) % 4
                hpe = p.batch("pe", [lambda t, jj=j, cc=ch, bb=bank: t.transpose(
                    out=self.pbf16(bb),
                    in_=o_tm[:, jj * 1024 + cc * 128: jj * 1024 + (cc + 1) * 128],
                    identity=self.ident_sb[:, :])],
                    waits=[h_osrc, self.h_init, bank_reader.get(bank)])
                hev = p.batch("act", [lambda a, jj=j, cc=ch, bb=bank: a.copy(
                    out=o_fm[:, cc * 2048 + jj * 128: cc * 2048 + (jj + 1) * 128],
                    in_=self.pbf16(bb))], waits=[hpe])
                bank_reader[bank] = hev
        h_ofm_done = hev
        h_add = {}
        at_reader = {}
        for mo in range(8):
            for tg in range(4):
                bank = 4 + ((mo * 4 + tg) % 2)
                fns = []
                for ch in range(8):
                    fns.append(lambda t, cc=ch, mm=mo, gg=tg, bb=bank: t.matmul(
                        out=self.P[bb][:, :],
                        lhsT=wo_bf[:, cc * 1024 + mm * 128:(cc * 1024 + (mm + 1) * 128)],
                        rhs=o_fm[:, cc * 2048 + gg * 512: cc * 2048 + (gg + 1) * 512],
                        start=(cc == 0), stop=(cc == 7)))
                hmm = p.batch("pe", fns, waits=[h_wo_ready, h_ofm_done,
                                                bank_reader.get(bank)])
                ys = (mo * 4 + tg) % 2
                hev = p.batch("act", [lambda a, bb=bank, y2=ys: a.copy(
                    out=at_sb[:, y2 * 512:(y2 + 1) * 512], in_=self.P[bb][:, :])],
                    waits=[hmm, at_reader.get(ys)])
                bank_reader[bank] = hev
                for tt in range(4):
                    j = tg * 4 + tt
                    bank2 = (tt % 2)
                    hpe = p.batch("pe", [lambda t, y2=ys, t3=tt, bb=bank2: t.transpose(
                        out=self.pbf16(bb),
                        in_=at_sb[:, y2 * 512 + t3 * 128: y2 * 512 + (t3 + 1) * 128],
                        identity=self.ident_sb[:, :])],
                        waits=[hev, bank_reader.get(bank2)])
                    at_reader[ys] = hpe
                    h_add[(mo, tg, tt)] = p.batch("dve", [
                        lambda v, jj=j, mm=mo, bb=bank2: v.tensor_tensor(
                            out=self.x_tm[:, jj * 1024 + mm * 128: jj * 1024 + (mm + 1) * 128],
                            in0=self.x_tm[:, jj * 1024 + mm * 128: jj * 1024 + (mm + 1) * 128],
                            in1=self.pbf16(bb), op=ALU.add)],
                        waits=[hpe, self.h_xtm_dma, h_add.get((mo - 1, tg, tt))])
                    bank_reader[bank2] = h_add[(mo, tg, tt)]
        self.h_w_tile = {tg2 * 4 + tt2: h_add[(7, tg2, tt2)]
                         for tg2 in range(4) for tt2 in range(4)}
        self.h_xtm = p.batch("dve", [lambda v: v.engine_nop()],
                             waits=[h_add[(7, tg2, tt2)] for tg2 in range(4) for tt2 in range(4)])
        if "E" not in self.phases and "G" not in self.phases:
            srcx2 = bass.AP(self.x_tm, 0, [[NOWN * D, 128], [D, 16], [1, D]])
            dstx2 = d["x_mid_d"][:, :].rearrange("(j p) d -> p j d", j=16, p=128)
            p.dma("xmid", [lambda sp: sp.dma_start(out=dstx2, in_=srcx2)],
                  waits=[self.h_xtm])

    def phase_d(self):
        nc, p, sb, d = self.nc, self.p, self.sb, self.dram
        self.xn1_own_fm = sb.at("xn1_own_fm", [128, 8 * 2048], BF16, OFF_PH)
        o = OFF_PH2 + 60416
        xc = sb.at("D_xc", [128, 2 * D], F32, o); o += 2 * 4096
        sqd = sb.at("D_sqd", [128, D], BF16, o); o += 2048
        xnt = sb.at("D_xnt", [128, 2 * D], BF16, o); o += 2 * 2048
        assert o <= PBYTES, o
        st_a, st_b, st_c = self.st_a, self.st_b, self.st_c
        h_stats = {}
        h_act1 = {}
        h_xn = {}
        h_pe = {}
        h_evac = {}
        h_w_tile = getattr(self, "h_w_tile", {})
        for j in range(16):
            par = j % 2
            h_x = h_w_tile.get(j, self.h_xtm)
            src = self.x_tm[:, j * D:(j + 1) * D]
            h_sqd2 = p.batch("act", [
                lambda a, srcc=src, pp=par: a.activation(
                    out=sqd[:, :], in_=srcc, func=AF.Square,
                    accum_out=st_b[:, pp:pp + 1]),
            ], waits=[h_x, h_xn.get(j - 2), self.h_ones])
            h_stats[j] = p.chain("dve", [
                lambda v, srcc=src, pp=par: v.reduce_sum(
                    out=st_a[:, 4 + pp:5 + pp], in_=srcc, axis=mybir.AxisListType.X),
                lambda v, pp=par: v.tensor_scalar_mul(
                    out=st_a[:, pp:pp + 1], in0=st_a[:, 4 + pp:5 + pp], scalar1=1.0 / D),
                lambda v, pp=par: v.scalar_tensor_tensor(
                    out=st_a[:, 2 + pp:3 + pp], in0=st_a[:, pp:pp + 1],
                    scalar=-1.0, in1=st_a[:, pp:pp + 1], op0=ALU.mult, op1=ALU.mult),
                lambda v, pp=par: v.tensor_scalar_add(
                    out=st_a[:, 6 + pp:7 + pp], in0=st_a[:, 2 + pp:3 + pp], scalar1=1e-5),
            ], waits=[h_x, h_xn.get(j - 2)])
            h_act1[j] = p.batch("act", [
                lambda a, pp=par: a.activation(
                    out=st_b[:, 4 + pp:5 + pp], in_=st_b[:, pp:pp + 1], func=AF.Sqrt,
                    scale=1.0 / D, bias=st_a[:, 6 + pp:7 + pp]),
            ], waits=[h_stats[j], h_sqd2])
            h_xn[j] = p.chain("dve", [
                lambda v, pp=par: v.reciprocal(
                    out=st_c[:, pp:pp + 1], in_=st_b[:, 4 + pp:5 + pp]),
                lambda v, pp=par: v.scalar_tensor_tensor(
                    out=st_c[:, 2 + pp:3 + pp], in0=st_a[:, pp:pp + 1],
                    scalar=-1.0, in1=st_c[:, pp:pp + 1], op0=ALU.mult, op1=ALU.mult),
                lambda v, srcc=src, pp=par: v.tensor_scalar(
                    out=xnt[:, pp * D:(pp + 1) * D], in0=srcc,
                    scalar1=st_c[:, pp:pp + 1], scalar2=st_c[:, 2 + pp:3 + pp],
                    op0=ALU.mult, op1=ALU.add),
            ], waits=[h_act1[j]])
            for ch in range(8):
                bank = ch % 4
                h_pe[(j, ch)] = p.batch("pe", [
                    lambda t, pp=par, cc=ch, bb=bank: t.transpose(
                        out=self.pbf16(bb),
                        in_=xnt[:, pp * D + cc * 128: pp * D + (cc + 1) * 128],
                        identity=self.ident_sb[:, :]),
                ], waits=[h_xn[j], self.h_init, self.h_pbank.get(bank)])
                dst = self.xn1_own_fm[:, ch * 2048 + j * 128: ch * 2048 + (j + 1) * 128]
                h_evac[(j, ch)] = p.batch("act", [
                    lambda a, dd2=dst, bb=bank: a.copy(out=dd2, in_=self.pbf16(bb)),
                ], waits=[h_pe[(j, ch)]])
                self.h_pbank[bank] = h_evac[(j, ch)]
            srcsb = bass.AP(self.xn1_own_fm, j * 128, [[16384, 128], [2048, 8], [1, 128]])
            dstd = bass.AP(d["xn1_fm_d"], j * 128, [[2048, 128], [128 * 2048, 8], [1, 128]])
            p.dma("d1st", [lambda sp, ss=srcsb, dd2=dstd: sp.dma_start(out=dd2, in_=ss)],
                  waits=[h_evac[(j, 7)]])
        self.h_d_tile = {j2: h_evac[(j2, 7)] for j2 in range(16)}
        self.h_xn1_ready = h_evac[(15, 7)]

    def phase_e(self):
        nc, p, sb, d = self.nc, self.p, self.sb, self.dram
        if "D" not in self.phases:
            self.load_x_tm("x_mid_d")
            self.xn1_own_fm = sb.at("xn1_own_fm", [128, 8 * 2048], BF16, OFF_PH)
            fns = []
            for ch in range(8):
                srcap = bass.AP(d["xn1_fm_d"], ch * 128 * 2048, [[2048, 128], [1, 2048]])
                dstap = self.xn1_own_fm[:, ch * 2048:(ch + 1) * 2048]
                fns.append(lambda sp, ss=srcap, dd2=dstap: sp.dma_start(out=dd2, in_=ss))
            self.h_xn1_ready = p.dma("xn1load", fns)
        h_xn1 = self.h_xn1_ready
        xn1 = self.xn1_own_fm
        o = OFF_PH2
        wst = sb.at("E_wst", [128, 2 * 512], F32, o); o += 4096
        wbf = sb.at("E_wbf", [128, 8 * 1024], BF16, o); o += 16384
        ktmp = sb.at("E_ktmp", [128, D], F32, o); o += 4096
        xs = sb.at("E_xs", [128, D], F32, o); o += 4096
        sms = sb.at("E_sms", [128, 192], F32, o); o += 768
        out_tm = sb.at("E_outtm", [128, 2 * D], BF16, o); o += 4096
        grp_q = sb.at("E_gq", [128, 128 * 64], BF16, o); o += 16384
        grp_k = sb.at("E_gk", [128, 128 * 64], BF16, o); o += 16384
        grp_v = sb.at("E_gv", [128, 128 * 64], BF16, o); o += 16384
        tds = sb.at("E_tds", [64, 2 * 1024], BF16, o); o += 4096
        msb = sb.at("E_msb", [128, 2 * 64], BF16, OFF_PH2 + 20480)
        mab = sb.at("E_mab", [128, 512], F32, OFF_PH2 + 20992)
        mblk = sb.at("E_mblk", [128, 512], BF16, OFF_PH2 + 23040)
        a1g = sb.at("E_a1g", [128, 2 * 512], BF16, o); o += 2048
        a1t = sb.at("E_a1t", [128, 2 * D], BF16, OFF_PH2)
        assert o <= PBYTES, o

        slot_sem = {}
        if "D" in self.phases:
            slot_sem[0] = self.h_xn1_ready
            slot_sem[1] = self.h_xn1_ready
        h_store = {}
        h_outuser = {}
        for wi, wname in enumerate(("WVT", "WKT", "WQT")):
            h_wc = {}
            for ch in range(8):
                for nb in range(2):
                    sl = (ch * 2 + nb) % 2
                    src = d[wname][ch * 128:(ch + 1) * 128, nb * 512:(nb + 1) * 512]
                    hld = p.dma(f"ew{sl}", [lambda sp, ss=src, s2=sl: sp.dma_start(
                        out=wst[:, s2 * 512:(s2 + 1) * 512], in_=ss)],
                        waits=[slot_sem.get(sl)])
                    h_wc[(ch, nb)] = p.batch("pool", [
                        lambda g, s2=sl, cc=ch, nn=nb: g.tensor_copy(
                            out=wbf[:, cc * 1024 + nn * 512: cc * 1024 + (nn + 1) * 512],
                            in_=wst[:, s2 * 512:(s2 + 1) * 512])],
                        waits=[hld, self.h_eproj_last.get((wi - 1, 15))])
                    slot_sem[sl] = h_wc[(ch, nb)]
            h_wrdy = h_wc[(7, 1)]
            for j in range(16):
                par = j % 2
                h_ev = {}
                for nb in range(2):
                    bank = nb
                    fns = []
                    for ch in range(8):
                        fns.append(lambda t, cc=ch, jj=j, nn=nb, bb=bank: t.matmul(
                            out=self.P[bb][:, :],
                            lhsT=xn1[:, cc * 2048 + jj * 128: cc * 2048 + (jj + 1) * 128],
                            rhs=wbf[:, cc * 1024 + nn * 512: cc * 1024 + (nn + 1) * 512],
                            start=(cc == 0), stop=(cc == 7)))
                    base_w = [getattr(self, "h_d_tile", {}).get(j, h_xn1),
                              self.h_pbank.get(bank)]
                    if j == 0:
                        for cc2 in range(8):
                            hmm2 = p.batch("pe", [fns[cc2]], waits=(
                                (base_w + [h_wc[(cc2, nb)]]) if cc2 == 0
                                else [h_wc[(cc2, nb)]]))
                    else:
                        hmm2 = p.batch("pe", fns, waits=[h_wrdy] + base_w)
                    self.h_eproj_last[(wi, j)] = hmm2
                    if wi == 0:
                        hv = p.batch("dve", [lambda v, bb=bank, pp=par, nn=nb: v.tensor_copy(
                            out=out_tm[:, pp * D + nn * 512: pp * D + (nn + 1) * 512],
                            in_=self.P[bb][:, :])],
                            waits=[hmm2, h_outuser.get(par)])
                    else:
                        hv = p.batch("dve", [lambda v, bb=bank, nn=nb: v.tensor_copy(
                            out=ktmp[:, nn * 512:(nn + 1) * 512],
                            in_=self.P[bb][:, :])],
                            waits=[hmm2, self.h_sm.get((wi, j - 1))])
                    self.h_pbank[bank] = hv
                    h_ev[nb] = hv
                if wi == 0:
                    srcv = out_tm[:, par * D:(par + 1) * D]
                    dstv = d["vv_d"][j * 128:(j + 1) * 128, :]
                    h_store[(wi, j)] = p.dma(
                        f"es{par}", [lambda sp, ss=srcv, dd2=dstv: sp.dma_start(
                            out=dd2, in_=ss)],
                        waits=[h_ev[0], h_ev[1]])
                    h_outuser[par] = h_store[(wi, j)]
                else:
                    if wi == 1:
                        rd_in = bass.AP(ktmp, 0, [[1024, 128], [1, 64], [64, 16]])
                        mxc = 64
                        b0 = bass.AP(sms, 0, [[192, 128], [0, 16], [1, 64]])
                        b1 = bass.AP(sms, 64, [[192, 128], [0, 16], [1, 64]])
                    else:
                        rd_in = bass.AP(ktmp, 0, [[1024, 128], [64, 16], [1, 64]])
                        mxc = 16
                        b0 = bass.AP(sms, 0, [[192, 128], [1, 16], [0, 64]])
                        b1 = bass.AP(sms, 64, [[192, 128], [1, 16], [0, 64]])
                    o3 = bass.AP(xs, 0, [[1024, 128], [64, 16], [1, 64]])
                    i3 = bass.AP(ktmp, 0, [[1024, 128], [64, 16], [1, 64]])
                    hsm1 = p.chain("dve", [
                        lambda v, r2=rd_in, mc=mxc: v.reduce_max(
                            out=sms[:, 0:mc], in_=r2, axis=mybir.AxisListType.X),
                        lambda v, mc=mxc: v.tensor_scalar_mul(
                            out=sms[:, 0:mc], in0=sms[:, 0:mc], scalar1=-1.0),
                        lambda v, o2=o3, i2=i3, bb0=b0: v.tensor_tensor(
                            out=o2, in0=i2, in1=bb0, op=ALU.add),
                    ], waits=[h_ev[0], h_ev[1]])
                    hsm2 = p.batch("act", [
                        lambda a: a.activation(out=xs[:, :], in_=xs[:, :],
                                               func=AF.Exp)], waits=[hsm1])
                    hsm3 = p.chain("dve", [
                        lambda v, r2=rd_in, mc=mxc: v.reduce_sum(
                            out=sms[:, 128:128 + mc],
                            in_=bass.AP(xs, 0, r2.ap), axis=mybir.AxisListType.X),
                        lambda v, mc=mxc: v.reciprocal(
                            out=sms[:, 64:64 + mc], in_=sms[:, 128:128 + mc]),
                        lambda v, i2=i3, bb1=b1, pp=par: v.tensor_tensor(
                            out=bass.AP(out_tm, pp * D, [[2 * D, 128], [64, 16], [1, 64]]),
                            in0=bass.AP(xs, 0, i2.ap),
                            in1=bb1, op=ALU.mult),
                    ], waits=[hsm2, h_outuser.get(par)])
                    self.h_sm[(wi, j)] = hsm3
                    srcv = out_tm[:, par * D:(par + 1) * D]
                    dstv = d[("sk_d" if wi == 1 else "sq_d")][j * 128:(j + 1) * 128, :]
                    h_store[(wi, j)] = p.dma(
                        f"es{par}", [lambda sp, ss=srcv, dd2=dstv: sp.dma_start(
                            out=dd2, in_=ss)],
                        waits=[hsm3])
                    h_outuser[par] = h_store[(wi, j)]
        h_qkv_done = [h_store[(wi, j)] for wi in range(3) for j in (14, 15)]

        bank_reader = dict(self.h_pbank)
        h_a1_store = {}
        h_ev4 = {}
        h_mext = {}
        h_mblk4 = {}
        h_a1ev = {}
        for half in range(2):
            fns = []
            for ti, tname in enumerate(("sq_d", "sk_d", "vv_d")):
                src = bass.AP(d[tname], half * 128 * 8192, [[64, 128], [8192, 128], [1, 64]])
                dsts = (grp_q, grp_k, grp_v)[ti]
                fns.append(lambda sp, ss=src, dd2=dsts: sp.dma_start(
                    out=dd2[:, :], in_=ss))
            h_bnc = p.dma(f"bn{half}", fns,
                          waits=list(h_qkv_done) + [self.h_mmb.get(half * 32 - 1)])
            for B4 in range(32):
                bp = B4 % 2
                g0 = B4 * 4
                gbat = half * 32 + B4
                tf = []
                for g4 in range(4):
                    tf.append(lambda t, gg=g0 + g4, pp=bp, g4b=g4: t.transpose(
                        out=self.pbf16(2 + pp, rows=64, cols=128, col0=g4b * 128),
                        in_=grp_q[:, gg * 64:(gg + 1) * 64],
                        identity=self.ident_sb[:, :]))
                    tf.append(lambda t, gg=g0 + g4, pp=bp, g4b=g4: t.transpose(
                        out=self.pbf16(2 + pp, rows=64, cols=128, col0=512 + g4b * 128),
                        in_=grp_k[:, gg * 64:(gg + 1) * 64],
                        identity=self.ident_sb[:, :]))
                htr = p.batch("pe", tf, waits=[h_bnc, h_ev4.get(gbat - 2), self.h_init])
                h_ev4[gbat] = p.batch("act", [
                    lambda a, pp=bp: a.copy(out=tds[:, pp * 1024:(pp + 1) * 1024],
                                            in_=self.pbf16(2 + pp, rows=64, cols=1024))],
                    waits=[htr, h_mext.get(gbat - 2)])
                maf = []
                for g4 in range(4):
                    maf.append(lambda t, pp=bp, g4b=g4: t.matmul(
                        out=self.P[4 + pp][:, g4b * 128:(g4b + 1) * 128],
                        lhsT=tds[:, pp * 1024 + 512 + g4b * 128: pp * 1024 + 512 + (g4b + 1) * 128],
                        rhs=tds[:, pp * 1024 + g4b * 128: pp * 1024 + (g4b + 1) * 128],
                        start=(g4b == 0), stop=True, skip_group_check=True))
                hma = p.batch("pe", maf, waits=[h_ev4[gbat], h_mext.get(gbat - 2)])
                h_mext[gbat] = p.chain("dve", [
                    lambda v, pp=bp: v.tensor_tensor(
                        out=mab[:, :], in0=self.P[4 + pp][:, 0:512],
                        in1=bass.AP(self.diagmask_sb, 0, [[128, 128], [0, 4], [1, 128]]),
                        op=ALU.mult),
                    lambda v, pp=bp: self._lowp(lambda: v.reduce_sum(
                        out=msb[:, pp * 64:(pp + 1) * 64],
                        in_=bass.AP(mab, 0, [[512, 128], [128, 4], [1, 16], [16, 8]]),
                        axis=mybir.AxisListType.X)),
                ], waits=[hma, h_mblk4.get(gbat - 2)])
                h_mblk4[gbat] = p.batch("dve", [lambda g2, pp=bp: g2.tensor_tensor(
                    out=mblk[:, :],
                    in0=bass.AP(msb, pp * 64, [[128, 128], [16, 4], [0, 8], [1, 16]]),
                    in1=bass.AP(self.diagmask_sb, 0, [[128, 128], [0, 4], [1, 128]]),
                    op=ALU.mult)],
                    waits=[h_mext[gbat], self.h_mmb.get(gbat - 1)])
                mbf = []
                for g4 in range(4):
                    mbf.append(lambda t, pp=bp, g4b=g4, gg=g0 + g4: t.matmul(
                        out=self.P[6 + pp][:, g4b * 64:(g4b + 1) * 64],
                        lhsT=mblk[:, g4b * 128:(g4b + 1) * 128],
                        rhs=grp_v[:, gg * 64:(gg + 1) * 64],
                        start=(g4b == 0), stop=True, skip_group_check=True))
                hb = p.batch("pe", mbf, waits=[h_mblk4[gbat], h_a1ev.get(gbat - 2)])
                self.h_mmb[gbat] = hb
                sl8 = (gbat // 2) % 2
                h_a1ev[gbat] = p.batch("act", [lambda a, pp=bp, s8=sl8, bb=B4: a.copy(
                    out=a1g[:, s8 * 512 + (bb % 2) * 256: s8 * 512 + (bb % 2 + 1) * 256],
                    in_=self.P[6 + pp][:, 0:256])],
                    waits=[hb, h_a1_store.get(gbat // 2 - 2)])
                if B4 % 2 == 1:
                    srca = bass.AP(a1g, sl8 * 512, [[1024, 128], [64, 8], [1, 64]])
                    dsta = bass.AP(d["a1_d"], (g0 - 4 + half * 128) * 8 * 1024,
                                   [[64, 128], [8192, 8], [1, 64]])
                    h_a1_store[gbat // 2] = p.dma(
                        f"ea{sl8}", [lambda sp, ss=srca, dd2=dsta: sp.dma_start(
                            out=dd2, in_=ss)], waits=[h_a1ev[gbat], h_a1ev[gbat - 1]])
        h_a1_all = [h_a1_store[z] for z in range(32)]

        h_add = {}
        for j in range(16):
            par = j % 2
            src = d["a1_d"][j * 128:(j + 1) * 128, :]
            hld = p.dma(f"ar{par}", [lambda sp, ss=src, pp=par: sp.dma_start(
                out=a1t[:, pp * D:(pp + 1) * D], in_=ss)],
                waits=h_a1_all + ([h_add.get(j - 2)] if j >= 2 else []))
            h_add[j] = p.batch("dve", [lambda v, jj=j, pp=par: v.tensor_tensor(
                out=self.x_tm[:, jj * D:(jj + 1) * D],
                in0=self.x_tm[:, jj * D:(jj + 1) * D],
                in1=a1t[:, pp * D:(pp + 1) * D], op=ALU.add)],
                waits=[hld, self.h_xtm, self.h_xtm_dma])
        self.h_xtm = p.batch("dve", [lambda v: v.nop()],
                             waits=[h_add[j] for j in range(16)])
        if "F" not in self.phases:
            srcx2 = bass.AP(self.x_tm, 0, [[NOWN * D, 128], [D, 16], [1, D]])
            dstx2 = d["x_l1_d"][:, :].rearrange("(j p) d -> p j d", j=16, p=128)
            p.dma("xl1", [lambda sp: sp.dma_start(out=dstx2, in_=srcx2)],
                  waits=[self.h_xtm])

    def phase_f(self):
        nc, p, sb, d = self.nc, self.p, self.sb, self.dram
        if "E" not in self.phases:
            self.load_x_tm("x_l1_d")
            self.xn1_own_fm = sb.at("xn1_own_fm", [128, 8 * 2048], BF16, OFF_PH)
            fns = []
            for ch in range(8):
                srcap = bass.AP(d["xn1_fm_d"], ch * 128 * 2048, [[2048, 128], [1, 2048]])
                dstap = self.xn1_own_fm[:, ch * 2048:(ch + 1) * 2048]
                fns.append(lambda sp, ss=srcap, dd2=dstap: sp.dma_start(out=dd2, in_=ss))
            self.h_xn1_ready = p.dma("xn1load2", fns)
        h_xn1 = self.h_xn1_ready
        xn1 = self.xn1_own_fm
        o = OFF_PH2
        fst = sb.at("F_fst", [128, 2 * 1024], I32, o); o += 2 * 4096
        wgq = sb.at("F_wgq", [128, 8 * 768], BF16, o); o += 12288
        wuq = sb.at("F_wuq", [128, 8 * 768], BF16, o); o += 12288
        wdq = sb.at("F_wdq", [128, 6 * 1024], BF16, o); o += 12288
        hq = sb.at("F_hq", [128, 6 * 2048], BF16, o); o += 24576
        sil = sb.at("F_sil", [128, 2 * 512], BF16, o); o += 2048
        assert o <= PBYTES, o

        slot_sem = {}
        if "E" in self.phases:
            slot_sem[0] = self.h_xtm
            slot_sem[1] = self.h_xtm
        h_gu_pe = {}
        h_hq = {}
        h_dn_pe = {}
        h_add = {}
        h_xadd = self.h_xtm
        for fq in range(4):
            h_cast_g = {}
            for wi, (wname, wq) in enumerate((("WGT", wgq), ("WUT", wuq))):
                for ch in range(8):
                    sl = ch % 2
                    src = d[wname][ch * 128:(ch + 1) * 128, fq * 768:(fq + 1) * 768]
                    hld = p.dma(f"fw{sl}", [lambda sp, ss=src, s2=sl: sp.dma_start(
                        out=fst[:, s2 * 1024:s2 * 1024 + 768], in_=ss)],
                        waits=[slot_sem.get(sl)])
                    h_cast_g[(wi, ch)] = p.batch("pool", [
                        lambda g, s2=sl, cc=ch, wq2=wq: g.tensor_copy(
                            out=wq2[:, cc * 768:(cc + 1) * 768],
                            in_=fst[:, s2 * 1024:s2 * 1024 + 768])],
                        waits=[hld, h_gu_pe.get((fq - 1, wi, 9))])
                    slot_sem[sl] = h_cast_g[(wi, ch)]
            h_gu_rdy = h_cast_g[(1, 7)]
            for fc in range(6):
                sl = fc % 2
                src = bass.AP(d["WDT"], (fq * 768 + fc * 128) * 1024, [[1024, 128], [1, 1024]])
                hld = p.dma(f"fw{sl}", [lambda sp, ss=src, s2=sl: sp.dma_start(
                    out=fst[:, s2 * 1024:(s2 + 1) * 1024], in_=ss)],
                    waits=[slot_sem.get(sl)])
                hcd = p.batch("pool", [
                    lambda g, s2=sl, cc=fc: g.tensor_copy(
                        out=wdq[:, cc * 1024:(cc + 1) * 1024],
                        in_=fst[:, s2 * 1024:(s2 + 1) * 1024])],
                    waits=[hld, h_dn_pe.get((fq - 1, 15, 1))])
                slot_sem[sl] = hcd
            h_wd_rdy = hcd
            for fc in range(6):
                for tg in range(4):
                    bank_g = 0 + ((fc * 4 + tg) % 2)
                    bank_u = 2 + ((fc * 4 + tg) % 2)
                    fns_g = []
                    fns_u = []
                    for ch in range(8):
                        fns_g.append(lambda t, cc=ch, ff=fc, gg=tg, bb=bank_g: t.matmul(
                            out=self.P[bb][:, :],
                            lhsT=wgq[:, cc * 768 + ff * 128: cc * 768 + (ff + 1) * 128],
                            rhs=xn1[:, cc * 2048 + gg * 512: cc * 2048 + (gg + 1) * 512],
                            start=(cc == 0), stop=(cc == 7)))
                        fns_u.append(lambda t, cc=ch, ff=fc, gg=tg, bb=bank_u: t.matmul(
                            out=self.P[bb][:, :],
                            lhsT=wuq[:, cc * 768 + ff * 128: cc * 768 + (ff + 1) * 128],
                            rhs=xn1[:, cc * 2048 + gg * 512: cc * 2048 + (gg + 1) * 512],
                            start=(cc == 0), stop=(cc == 7)))
                    if fc == 0 and tg == 0:
                        for cc2 in range(8):
                            hg = p.batch("pe", [fns_g[cc2]], waits=(
                                ([h_xn1, self.h_pbank.get(("fg", bank_g))]
                                 if cc2 == 0 else []) + [h_cast_g[(0, cc2)]]))
                        for cc2 in range(8):
                            hu = p.batch("pe", [fns_u[cc2]], waits=(
                                ([self.h_pbank.get(("fu", bank_u))]
                                 if cc2 == 0 else []) + [h_cast_g[(1, cc2)]]))
                    else:
                        hg = p.batch("pe", fns_g, waits=[h_gu_rdy, h_xn1,
                                                         self.h_pbank.get(("fg", bank_g))])
                        hu = p.batch("pe", fns_u, waits=[self.h_pbank.get(("fu", bank_u))])
                    h_gu_pe[(fq, 0, 9)] = hg
                    h_gu_pe[(fq, 1, 9)] = hu
                    ys = (fc * 4 + tg) % 2
                    hs = p.batch("act", [lambda a, bb=bank_g, y2=ys: a.activation(
                        out=sil[:, y2 * 512:(y2 + 1) * 512], in_=self.P[bb][:, :],
                        func=AF.Silu, scale=1.0 / 127.0)],
                        waits=[hg, h_hq.get((fq, fc * 4 + tg - 2))])
                    h_hq[(fq, fc * 4 + tg)] = p.batch("dve", [
                        lambda v, bb=bank_u, y2=ys, ff=fc, gg=tg: v.scalar_tensor_tensor(
                            out=hq[:, ff * 2048 + gg * 512: ff * 2048 + (gg + 1) * 512],
                            in0=self.P[bb][:, :], scalar=1.0 / 127.0,
                            in1=sil[:, y2 * 512:(y2 + 1) * 512],
                            op0=ALU.mult, op1=ALU.mult)],
                        waits=[hs, hu, h_dn_pe.get((fq - 1, 15, 1))])
                    self.h_pbank[("fg", bank_g)] = hs
                    self.h_pbank[("fu", bank_u)] = h_hq[(fq, fc * 4 + tg)]
            h_h_all = [h_hq[(fq, z)] for z in range(24)]
            for j in range(16):
                for mo in range(2):
                    bank = 4 + ((j * 2 + mo) % 2)
                    fns = []
                    for fc in range(6):
                        fns.append(lambda t, cc=fc, jo=j, mm=mo, bb=bank: t.matmul(
                            out=self.P[bb][:, :],
                            lhsT=hq[:, cc * 2048 + jo * 128: cc * 2048 + (jo + 1) * 128],
                            rhs=wdq[:, cc * 1024 + mm * 512: cc * 1024 + (mm + 1) * 512],
                            start=(cc == 0), stop=(cc == 5)))
                    hdp = p.batch("pe", fns, waits=[h_wd_rdy,
                                                    self.h_pbank.get(("fd", bank))] + h_h_all)
                    h_dn_pe[(fq, j, mo)] = hdp
                    h_add[(fq, j, mo)] = p.batch("dve", [
                        lambda v, jj=j, mm=mo, bb=bank: v.scalar_tensor_tensor(
                            out=self.x_tm[:, jj * 1024 + mm * 512: jj * 1024 + (mm + 1) * 512],
                            in0=self.P[bb][:, :], scalar=1.0 / 127.0,
                            in1=self.x_tm[:, jj * 1024 + mm * 512: jj * 1024 + (mm + 1) * 512],
                            op0=ALU.mult, op1=ALU.add)],
                        waits=[hdp, h_xadd, self.h_xtm_dma, h_add.get((fq - 1, j, mo))])
                    self.h_pbank[("fd", bank)] = h_add[(fq, j, mo)]
        self.h_xtm = p.batch("dve", [lambda v: v.nop()],
                             waits=[h_add[(3, j, mo)] for j in range(16) for mo in range(2)])

    def phase_g(self):
        p, d = self.p, self.dram
        src = bass.AP(self.x_tm, 0, [[NOWN * D, 128], [D, 16], [1, D]])
        dst = d["out_own"][:, :].rearrange("(j p) d -> p j d", j=16, p=128)
        p.dma("out", [lambda sp: sp.dma_start(out=dst, in_=src)],
              waits=[self.h_xtm])

    def finish(self):
        self.p.final_wait()
        sems_ctx = []
        nc = self.nc
        sems = {}
        for name in ENGS:
            sems[name] = nc.alloc_semaphore(name=f"sem_{name}")
        for dsem in self.p.ndma:
            sems["dma:" + dsem] = nc.alloc_semaphore(name=f"dsem_{dsem}")
        with nc.Block() as block:
            self.p.realize(block, sems)
        return nc


def build_nc(phases, debug_outputs=()):
    b = Builder(phases, debug_outputs)
    b.load_consts()
    for ph in phases:
        if ph == "A":
            b.phase_a()
        elif ph == "G":
            b.phase_g()
        else:
            getattr(b, f"phase_{ph.lower()}")()
    b.finish()
    return b


def kernel(**inputs):
    in_maps = host_shards(inputs)
    results = [dict() for _ in range(8)]
    for phases in LAUNCH_PLAN:
        b = build_nc(phases)
        lm = []
        for c in range(8):
            mm = {}
            for name, kind in b.kinds.items():
                if kind != "ExternalInput":
                    continue
                if name in in_maps[c]:
                    mm[name] = in_maps[c][name]
                elif name in results[c]:
                    mm[name] = results[c][name]
                else:
                    raise KeyError(f"missing input {name}")
            lm.append(mm)
        res = run_bass_kernel_spmd(b.nc, lm, core_ids=list(range(8)))
        for c in range(8):
            results[c].update(res.results[c])
    return host_gather(results)


# version 13 (final)
# speedup vs baseline: 1.1783x; 1.1783x over previous
import numpy as np
import ml_dtypes
import concourse.bass as bass
import concourse.mybir as mybir
from concourse.bass_utils import run_bass_kernel_spmd

F32 = mybir.dt.float32
BF16 = mybir.dt.bfloat16
I32 = mybir.dt.int32
AF = mybir.ActivationFunctionType
ALU = mybir.AluOpType
BF = ml_dtypes.bfloat16

D = 1024
S = 4096
NOWN = 16
NGRP = 8
SB_BASE = 16512
PBYTES = 211328

C_NHP = 8
C_NGRP = 8
C_CUT = set()
FUSED = True
LAUNCH_PLAN = ([["A", "B", "C", "W", "D", "E", "F", "G"]] if FUSED
               else [["A"], ["B"], ["C"], ["W", "D"], ["E"], ["F", "G"]])


def own_global_tiles(sub):
    return [4 * i + 2 * sub + j for i in range(8) for j in range(2)]


def slot_global(s, sub):
    u = s // 2
    if s % 2 == 0:
        return 4 * (u // 2) + 2 * sub + (u % 2)
    return 4 * (u // 2) + 2 * (1 - sub) + (u % 2)


def build_masks(sub):
    masks = np.zeros((NGRP, 4, 128, 256), np.float32)
    for i in range(NGRP):
        gq = [4 * i + 2 * sub, 4 * i + 2 * sub + 1]
        qt_g = np.concatenate([g * 128 + np.arange(128) for g in gq])[None, :]
        for kk in range(4):
            gk = slot_global(4 * i + kk, sub)
            kt_g = gk * 128 + np.arange(128)[:, None]
            masks[i, kk] = (kt_g <= qt_g).astype(np.float32)
    return masks.astype(BF)


def host_shards(inputs):
    x = np.ascontiguousarray(np.asarray(inputs["L0_attn_x"], dtype=np.float32))
    ident = np.eye(128).astype(BF)
    tt = (np.arange(128)[:, None] // 16) == (np.arange(512)[None, :] // 64)
    vblkmask = tt.astype(np.float32).astype(BF)
    dd = (np.arange(128)[:, None] // 16) == (np.arange(128)[None, :] // 16)
    diagmask = dd.astype(np.float32).astype(BF)
    shared = dict(
        Wq=np.ascontiguousarray(np.asarray(inputs["L0_attn_Wq"], np.float32)),
        Wk=np.ascontiguousarray(np.asarray(inputs["L0_attn_Wk"], np.float32)),
        Wv=np.ascontiguousarray(np.asarray(inputs["L0_attn_Wv"], np.float32)),
        Wo=np.ascontiguousarray(np.asarray(inputs["L0_attn_Wo"], np.float32)),
        lq1=np.asarray(inputs["L0_attn_lq1"], np.float32).reshape(1, 64).copy(),
        lk1=np.asarray(inputs["L0_attn_lk1"], np.float32).reshape(1, 64).copy(),
        lq2=np.asarray(inputs["L0_attn_lq2"], np.float32).reshape(1, 64).copy(),
        lk2=np.asarray(inputs["L0_attn_lk2"], np.float32).reshape(1, 64).copy(),
        Ws1=np.ascontiguousarray(np.asarray(inputs["L0_ffn_Ws1"], np.float32)),
        Ws2=np.ascontiguousarray(np.asarray(inputs["L0_ffn_Ws2"], np.float32)),
        W1=np.ascontiguousarray(np.asarray(inputs["L0_ffn_W1"], np.float32)),
        W2=np.ascontiguousarray(np.asarray(inputs["L0_ffn_W2"], np.float32)),
        Wr=np.ascontiguousarray(np.asarray(inputs["L0_ffn_Wr"], np.float32)),
        WQT=np.ascontiguousarray(np.asarray(inputs["L1_attn_W_Q"], np.float32).T),
        WKT=np.ascontiguousarray(np.asarray(inputs["L1_attn_W_K"], np.float32).T),
        WVT=np.ascontiguousarray(np.asarray(inputs["L1_attn_W_V"], np.float32).T),
        WGT=np.ascontiguousarray(np.asarray(inputs["L1_ffn_w_gate"]).T.astype(np.int32)),
        WUT=np.ascontiguousarray(np.asarray(inputs["L1_ffn_w_up"]).T.astype(np.int32)),
        WDT=np.ascontiguousarray(np.asarray(inputs["L1_ffn_w_down"]).T.astype(np.int32)),
        ident=ident,
        vblkmask=vblkmask, diagmask=diagmask,
    )
    lampack = np.zeros((128, 2), np.float32)
    lampack[0:64, 0] = np.asarray(inputs["L0_attn_lq1"], np.float32)
    lampack[64:128, 0] = np.asarray(inputs["L0_attn_lq2"], np.float32)
    lampack[0:64, 1] = np.asarray(inputs["L0_attn_lk1"], np.float32)
    lampack[64:128, 1] = np.asarray(inputs["L0_attn_lk2"], np.float32)
    shared["identf"] = np.ascontiguousarray(
        np.concatenate([np.eye(128, dtype=np.float32), lampack], axis=1))
    in_maps = []
    for c in range(8):
        b, sub = c // 2, c % 2
        own = own_global_tiles(sub)
        oth = own_global_tiles(1 - sub)
        m = dict(shared)
        m["x_own"] = np.ascontiguousarray(
            np.concatenate([x[b, t * 128:(t + 1) * 128] for t in own], axis=0))
        m["x_other"] = np.ascontiguousarray(
            np.concatenate([x[b, t * 128:(t + 1) * 128] for t in oth], axis=0))
        m["masks"] = np.ascontiguousarray(build_masks(sub))
        in_maps.append(m)
    return in_maps


def host_gather(results):
    out = np.zeros((4, S, D), np.float32)
    for c in range(8):
        b, sub = c // 2, c % 2
        oo = results[c]["out_own"]
        for j, t in enumerate(own_global_tiles(sub)):
            out[b, t * 128:(t + 1) * 128] = oo[j * 128:(j + 1) * 128]
    return out



ENGS = ("pe", "act", "dve", "pool", "sp")


class H:
    __slots__ = ("kind", "n")

    def __init__(self, kind, n):
        self.kind = kind
        self.n = n


class Prog:

    def __init__(self, nc):
        self.nc = nc
        self.ops = {e: [] for e in ENGS}
        self.nbatch = {e: 0 for e in ENGS}
        self.ndma = {}
        self.waited = {e: {} for e in ENGS}

    def batch(self, eng, fns, waits=()):
        w = self.waited[eng]
        for h in waits:
            if h is None:
                continue
            val = h.n * (16 if h.kind.startswith("dma:") else 1)
            if w.get(h.kind, 0) >= val:
                continue
            w[h.kind] = val
            self.ops[eng].append(("wait", h.kind, val))
        for fn in fns[:-1]:
            self.ops[eng].append(("call", fn))
        self.ops[eng].append(("inc", fns[-1]))
        self.nbatch[eng] += 1
        return H(eng, self.nbatch[eng])

    def chain(self, eng, fns, waits=()):
        h = None
        for i, fn in enumerate(fns):
            h = self.batch(eng, [fn], waits=(list(waits) + [h]) if i == 0 else [h])
        return h

    def dma(self, sem, fns, waits=(), eng="sp"):
        w = self.waited[eng]
        for h in waits:
            if h is None:
                continue
            val = h.n * (16 if h.kind.startswith("dma:") else 1)
            if w.get(h.kind, 0) >= val:
                continue
            w[h.kind] = val
            self.ops[eng].append(("wait", h.kind, val))
        for fn in fns:
            self.ops[eng].append(("dma", sem, fn))
            self.ndma[sem] = self.ndma.get(sem, 0) + 1
        return H("dma:" + sem, self.ndma[sem])

    def final_wait(self):
        for sem, n in self.ndma.items():
            self.ops["sp"].append(("wait", "dma:" + sem, 16 * n))

    def realize(self, block, sems):
        def body(name):
            def run(eng):
                for item in self.ops[name]:
                    tag = item[0]
                    if tag == "wait":
                        eng.wait_ge(sems[item[1]], item[2])
                    elif tag == "call":
                        item[1](eng)
                    elif tag == "inc":
                        item[1](eng).then_inc(sems[name], 1)
                    elif tag == "dma":
                        item[2](eng).then_inc(sems["dma:" + item[1]], 16)
            return run
        block.tensor(body("pe"))
        block.scalar(body("act"))
        block.vector(body("dve"))
        block.gpsimd(body("pool"))
        block.sync(body("sp"))


class SbufMap:
    def __init__(self, nc):
        self.nc = nc

    def at(self, name, shape, dtype, off):
        per_part = int(np.prod(shape[1:])) * mybir.dt.size(dtype)
        assert off + per_part <= PBYTES, f"{name}: {off}+{per_part} > {PBYTES}"
        assert off % 32 == 0, name
        return self.nc.alloc_sbuf_tensor_at(name, shape, dtype, offset=SB_BASE + off)


OFF_XTM = 0
OFF_CONST = 65536
OFF_SMALL = 86016
OFF_PH = 88064
OFF_PH2 = 120832



class Builder:
    def __init__(self, phases, debug_outputs=()):
        self.nc = bass.Bass("TRN2", target_bir_lowering=False)
        self.phases = list(phases)
        self.debug_outputs = set(debug_outputs)
        self.p = Prog(self.nc)
        self.sb = SbufMap(self.nc)
        self.dram = {}
        self.kinds = {}
        self._declare_dram()
        self._alloc_static()
        self.h_xtm = None
        self.h_xtm_dma = None
        self.h_router_done = {}
        self.h_eproj_last = {}
        self.h_sm = {}
        self.h_mblk = {}
        self.h_mmb = {}
        self.h_gate_evac = {}
        self.h_sq = {}
        self.h_a_lastread = None

    def _dram(self, name, shape, dtype, kind):
        t = self.nc.dram_tensor(name, list(shape), dtype, kind=kind)
        self.dram[name] = t
        self.kinds[name] = kind
        return t

    def _declare_dram(self):
        E, O, I = "ExternalInput", "ExternalOutput", "Internal"
        ph = self.phases
        self._dram("ident", (128, 128), BF16, E)
        self._dram("identf", (128, 130), F32, E)
        if "A" in ph or ("B" in ph and "A" not in ph):
            self._dram("x_own", (2048, D), F32, E)
        if "A" in ph:
            self._dram("x_other", (2048, D), F32, E)
        if "B" in ph:
            for n, sh in (("Ws1", (D, 512)), ("Ws2", (512, D)), ("W1", (8, D, 512)),
                          ("W2", (8, 512, D)), ("Wr", (D, 8))):
                self._dram(n, sh, F32, E)
        if "C" in ph:
            for n in ("Wq", "Wk", "Wv"):
                self._dram(n, (D, 2048), F32, E)
            self._dram("masks", (NGRP, 4, 128, 256), BF16, E)
        if "W" in ph:
            self._dram("Wo", (D, D), F32, E)
        if "E" in ph:
            for n in ("WQT", "WKT", "WVT"):
                self._dram(n, (D, D), F32, E)
            self._dram("vblkmask", (128, 512), BF16, E)
            self._dram("diagmask", (128, 128), BF16, E)
        if "F" in ph:
            for n in ("WGT", "WUT"):
                self._dram(n, (D, 3072), I32, E)
            self._dram("WDT", (3072, D), I32, E)

        def hand(name, shape, dtype, prod, users):
            produced = prod in ph
            used = any(u in ph for u in users)
            if name in self.debug_outputs:
                kind = O
            elif produced and used:
                kind = I
            elif produced:
                kind = O
            elif used:
                kind = E
            else:
                return
            self._dram(name, shape, dtype, kind)

        hand("xn_fm_d", (D, S), BF16, "A", ("B", "C"))
        hand("x_spill_d", (2048, D), F32, "B", ("W",))
        if "C" in ph and "W" not in ph:
            self._dram("o_tm_d", (2048, D), BF16, O)
        elif "W" in ph and "C" not in ph:
            self._dram("o_tm_d", (2048, D), BF16, E)
        hand("x_mid_d", (2048, D), F32, "W", ("E",))
        hand("xn1_fm_d", (D, 2048), BF16, "D", ("E", "F"))
        if "E" in ph:
            for n in ("sq_d", "sk_d", "vv_d", "a1_d"):
                self._dram(n, (2048, D), BF16, I)
            hand("x_l1_d", (2048, D), F32, "E", ("F",))
        elif "F" in ph:
            hand("x_l1_d", (2048, D), F32, "E", ("F",))
        if "G" in ph:
            self._dram("out_own", (2048, D), F32, O)

    def _alloc_static(self):
        sb = self.sb
        self.x_tm = sb.at("x_tm", [128, NOWN * D], F32, OFF_XTM)
        self.xn_full = sb.at("xn_full", [128, 32 * 1024], BF16, OFF_XTM)
        o = OFF_CONST
        self.ident_sb = sb.at("ident_sb", [128, 128], BF16, o); o += 256
        self.identf_sb = sb.at("identf_sb", [128, 130], F32, o); o += 544
        self.ones_bf = sb.at("ones_bf", [128, 1], BF16, o); o += 32
        self.masks_sb = sb.at("masks_sb", [128, NGRP * 4 * 256], BF16, o); o += NGRP * 4 * 512
        self.vblkmask_sb = sb.at("vblkmask_sb", [128, 512], BF16, o); o += 1024
        self.diagmask_sb = sb.at("diagmask_sb", [128, 128], BF16, o); o += 256
        self.neglam_bc = sb.at("neglam_bc", [128, 1], F32, o); o += 32
        self.lam_sc = sb.at("lam_sc", [1, 4], F32, o); o += 32
        self.prob_sb = sb.at("prob_sb", [128, NOWN * 8], F32, o); o += NOWN * 32
        assert o <= OFF_SMALL, o
        o = OFF_SMALL
        self.eps_sb = sb.at("eps_sb", [128, 1], F32, o); o += 32
        self.st_a = sb.at("st_a", [128, 16], F32, o); o += 64
        self.st_b = sb.at("st_b", [128, 16], F32, o); o += 64
        self.st_c = sb.at("st_c", [128, 16], F32, o); o += 64
        self.P01 = self.nc.alloc_psum_tensor("P01", [128, 1024], F32)
        self.P23 = self.nc.alloc_psum_tensor("P23", [128, 1024], F32)
        self.Pbig = [self.P01, self.P23]
        self.P = [self.P01[:, 0:512], self.P01[:, 512:1024],
                  self.P23[:, 0:512], self.P23[:, 512:1024]]
        self.P += [self.nc.alloc_psum_tensor(f"P{i}", [128, 512], F32) for i in range(4, 8)]
        self.h_pbank = {}

    def _lowp(self, fn):
        with self.nc.allow_low_precision(reason="single-nonzero-term reduce"):
            return fn()

    def pbf16(self, i, rows=128, cols=128, row0=0, col0=0):
        return self.P[i][:, :].bitcast(BF16)[row0:row0 + rows, col0:col0 + cols]

    def load_consts(self):
        p, d = self.p, self.dram
        fns = [lambda sp: sp.dma_start(out=self.ident_sb[:, :], in_=d["ident"][:, :]),
               lambda sp: sp.dma_start(out=self.identf_sb[:, :], in_=d["identf"][:, :])]
        if "masks" in d:
            fns.append(lambda sp: sp.dma_start(
                out=bass.AP(self.masks_sb, 0, [[8192, 128], [1024, 8], [256, 4], [1, 256]]),
                in_=d["masks"][:, :, :, :].rearrange("g k p q -> p g k q")))
        if "vblkmask" in d:
            fns.append(lambda sp: sp.dma_start(out=self.vblkmask_sb[:, :], in_=d["vblkmask"][:, :]))
            fns.append(lambda sp: sp.dma_start(out=self.diagmask_sb[:, :], in_=d["diagmask"][:, :]))
        self.h_consts = p.dma("init", fns)
        self.h_init = self.h_consts
        self.h_pe_warm = p.batch("pe", [lambda t: t.matmul(
            out=self.P[7][:, 0:128], lhsT=self.ident_sb[:, :], rhs=self.ident_sb[:, :],
            start=True, stop=True)], waits=[self.h_consts])
        self.h_ones = p.batch("dve", [
            lambda v: v.memset(self.ones_bf[:, :], 1.0),
            lambda v: v.memset(self.eps_sb[:, :], 1e-5),
        ])
        self.h_ones_eps = self.h_ones
        self.h_prew1 = None
        if "A" in self.phases and "B" in self.phases:
            self.prew1 = self.sb.at("preW1", [128, 4096], F32, OFF_PH2 + 49536)
            pfns = []
            for q4 in range(2):
                srcw = bass.AP(d["W1"], q4 * 4 * 65536, [[512, 128], [65536, 4], [1, 512]])
                dstw = bass.AP(self.prew1, q4 * 2048, [[4096, 128], [512, 4], [1, 512]])
                pfns.append(lambda sp, sw=srcw, dw=dstw: sp.dma_start(out=dw, in_=sw))
            self.h_prew1 = p.dma("pw", pfns)

    def load_x_tm(self, src_name):
        p, d = self.p, self.dram
        src = d[src_name][:, :].rearrange("(j p) d -> p j d", j=16, p=128)
        dst = bass.AP(self.x_tm, 0, [[NOWN * D, 128], [D, 16], [1, D]])
        h = p.dma("xload", [lambda sp: sp.dma_start(out=dst, in_=src)])
        self.h_xtm_dma = h
        self.h_xtm = h
        return h

    def phase_a(self):
        nc, p, sb, d = self.nc, self.p, self.sb, self.dram
        self.xn_own_fm = sb.at("xn_own_fm", [128, 8 * 2048], BF16, OFF_PH)
        o = OFF_PH2
        xa = sb.at("A_xa", [128, 4 * D], F32, o); o += 4 * 4096
        xc = sb.at("A_xc", [128, 2 * D], F32, o); o += 2 * 4096
        sqd = sb.at("A_sqd", [128, 2 * D], BF16, o); o += 2 * 2048
        xnt = sb.at("A_xnt", [128, 4 * D], BF16, o); o += 4 * 2048
        xst = sb.at("A_xst", [128, 2 * D], BF16, o); o += 2 * 2048
        assert o <= PBYTES

        h_load = self.load_x_tm("x_own")
        st_a, st_b, st_c = self.st_a, self.st_b, self.st_c
        h_stats = {}
        h_sqd = {}
        h_act1 = {}
        h_xn = {}
        h_pe = {}
        h_evac = {}
        h_wr = {}
        h_oth = {}
        for s in range(32):
            par = s % 4
            u = s // 2
            up = u % 4
            if s % 2 == 1:
                waits = [self.h_init]
                if s >= 8:
                    waits.append(h_xn.get(s - 8))
                    waits.append(h_sqd.get(s - 8))
                src_d = d["x_other"][:, :].rearrange("(j p) d -> p j d", j=16, p=128)[:, u]
                h_oth[s] = p.dma(
                    f"xa{up}",
                    [lambda sp, sd=src_d, pp=up: sp.dma_start(
                        out=xa[:, pp * D:(pp + 1) * D], in_=sd)],
                    waits=waits)
                src = xa[:, up * D:(up + 1) * D]
                ld_wait = h_oth[s]
            else:
                src = self.x_tm[:, u * D:(u + 1) * D]
                ld_wait = h_load
            sq2 = s % 2
            h_sqd[s] = p.batch("act", [
                lambda a, srcc=src, pp=par, q2=sq2: a.activation(
                    out=sqd[:, q2 * D:(q2 + 1) * D], in_=srcc, func=AF.Square,
                    accum_out=st_b[:, pp:pp + 1]),
            ], waits=[ld_wait, h_xn.get(s - 4), self.h_ones, h_sqd.get(s - 2)])
            h_stats[s] = p.chain("dve", [
                lambda v, srcc=src, pp=par: v.reduce_sum(
                    out=st_a[:, 8 + pp:9 + pp], in_=srcc, axis=mybir.AxisListType.X),
                lambda v, pp=par: v.tensor_scalar_mul(
                    out=st_a[:, pp:pp + 1], in0=st_a[:, 8 + pp:9 + pp], scalar1=1.0 / D),
                lambda v, pp=par: v.scalar_tensor_tensor(
                    out=st_a[:, 4 + pp:5 + pp], in0=st_a[:, pp:pp + 1],
                    scalar=-1.0, in1=st_a[:, pp:pp + 1], op0=ALU.mult, op1=ALU.mult),
                lambda v, pp=par: v.tensor_scalar_add(
                    out=st_a[:, 12 + pp:13 + pp], in0=st_a[:, 4 + pp:5 + pp], scalar1=1e-5),
            ], waits=[ld_wait, h_xn.get(s - 4)])
            h_act1[s] = p.batch("act", [
                lambda a, pp=par: a.activation(
                    out=st_b[:, 8 + pp:9 + pp], in_=st_b[:, pp:pp + 1], func=AF.Sqrt,
                    scale=1.0 / D, bias=st_a[:, 12 + pp:13 + pp]),
            ], waits=[h_stats[s], h_sqd[s]])
            h_xn[s] = p.chain("dve", [
                lambda v, pp=par: v.reciprocal(
                    out=st_c[:, pp:pp + 1], in_=st_b[:, 8 + pp:9 + pp]),
                lambda v, pp=par: v.scalar_tensor_tensor(
                    out=st_c[:, 4 + pp:5 + pp], in0=st_a[:, pp:pp + 1],
                    scalar=-1.0, in1=st_c[:, pp:pp + 1], op0=ALU.mult, op1=ALU.mult),
                lambda v, srcc=src, pp=par: v.tensor_scalar(
                    out=xnt[:, pp * D:(pp + 1) * D], in0=srcc,
                    scalar1=st_c[:, pp:pp + 1], scalar2=st_c[:, 4 + pp:5 + pp],
                    op0=ALU.mult, op1=ALU.add),
            ], waits=[h_act1[s]])
            for ch in range(8):
                bank = ch % 4
                waits = [h_xn[s], self.h_init, self.h_pbank.get(bank)]
                h_pe[(s, ch)] = p.batch("pe", [
                    lambda t, pp=par, cc=ch, bb=bank: t.transpose(
                        out=self.pbf16(bb),
                        in_=xnt[:, pp * D + cc * 128: pp * D + (cc + 1) * 128],
                        identity=self.ident_sb[:, :]),
                ], waits=waits)
                if s % 2 == 0:
                    dst = self.xn_own_fm[:, ch * 2048 + u * 128: ch * 2048 + (u + 1) * 128]
                else:
                    dst = xst[:, (u % 2) * D + ch * 128: (u % 2) * D + (ch + 1) * 128]
                h_evac[(s, ch)] = p.batch("act", [
                    lambda a, dd2=dst, bb=bank: a.copy(out=dd2, in_=self.pbf16(bb)),
                ], waits=[h_pe[(s, ch)]] + ([h_wr.get(s - 4)] if s % 2 == 1 else []))
                self.h_pbank[bank] = h_evac[(s, ch)]
            if s % 2 == 0:
                srcsb = bass.AP(self.xn_own_fm, u * 128, [[16384, 128], [2048, 8], [1, 128]])
            else:
                srcsb = bass.AP(xst, (u % 2) * D, [[2 * D, 128], [128, 8], [1, 128]])
            dstd = bass.AP(d["xn_fm_d"], s * 128, [[S, 128], [128 * S, 8], [1, 128]])
            sem = "stO" if s % 2 == 0 else f"st{u % 2}"
            h_wr[s] = p.dma(sem, [
                lambda sp, ss=srcsb, dd2=dstd: sp.dma_start(out=dd2, in_=ss)],
                waits=[h_evac[(s, 7)]])
        self.h_xn_fm_written = None
        self.h_a_lastread = h_stats[30]
        self.h_xn_own_ready = h_evac[(30, 7)]
        self.h_a_xa = [h_sqd[31], h_stats[31], h_oth[31]]
        self.h_a_xnt = [h_sqd[31], h_pe[(31, 7)]]
        self.h_a_xst = [h_pe[(31, 7)], h_wr[29], h_wr[31]]
        self.h_a_xn_stores = [h_wr[30], h_wr[29], h_wr[31]]

    def phase_b(self):
        nc, p, sb, d = self.nc, self.p, self.sb, self.dram
        if "A" not in self.phases:
            self.load_x_tm("x_own")
            self.xn_own_fm = sb.at("xn_own_fm", [128, 8 * 2048], BF16, OFF_PH)
            fns = []
            for ch in range(8):
                srcap = bass.AP(d["xn_fm_d"], ch * 128 * S, [[S, 128], [256, 16], [1, 128]])
                dstap = bass.AP(self.xn_own_fm, ch * 2048, [[16384, 128], [128, 16], [1, 128]])
                fns.append(lambda sp, ss=srcap, dd2=dstap: sp.dma_start(out=dd2, in_=ss))
            h = p.dma("xnload", fns)
            self.h_xn_own_ready = h
            h_xtm_readers = None
        else:
            h_xtm_readers = self.h_a_lastread
        o = OFF_PH2
        wrst = sb.at("B_wrst", [128, 64], F32, o); o += 256
        wrbf = sb.at("B_wrbf", [128, 64], BF16, o); o += 128
        wst = sb.at("B_wst", [128, 2 * 2048], F32, o); o += 16384
        w1bf = sb.at("B_w1bf", [128, 2 * 4096], BF16, o); o += 16384
        w2bf = sb.at("B_w2bf", [128, 2 * 4096], BF16, o); o += 16384
        he = sb.at("B_he", [128, 2 * 4 * 2048], BF16, o); o += 32768
        rtile = sb.at("B_rtile", [128, 2 * 512], BF16, o); o += 2048
        ytile = sb.at("B_ytile", [128, 2 * 512], BF16, o); o += 2048
        lg = sb.at("B_lg", [128, 8], F32, o); o += 32
        q8 = sb.at("B_q8", [128, 8], F32, o); o += 32
        rt = sb.at("B_rt", [128, 8], F32, o); o += 32
        assert o <= PBYTES, o

        xn = self.xn_own_fm
        h_xn = getattr(self, "h_xn_own_ready", None)
        bank_reader = {}
        stage_slot = {}
        h_gate_last = {}
        h_down_last = {}
        h_prob = {}

        a_xa = getattr(self, "h_a_xa", None) or []
        a_w1 = getattr(self, "h_a_xnt", None) or []
        a_w2 = getattr(self, "h_a_xst", None) or []

        srcwr = bass.AP(d["Wr"], 0, [[8, 128], [1024, 8], [1, 8]])
        dstwr = bass.AP(wrst, 0, [[64, 128], [8, 8], [1, 8]])
        h_wr = p.dma("wrload", [lambda sp: sp.dma_start(out=dstwr, in_=srcwr)],
                     waits=a_xa)
        h_wrbf = p.batch("pool", [lambda g: g.tensor_copy(out=wrbf[:, :], in_=wrst[:, :])],
                         waits=[h_wr])

        for j in range(16):
            bank = j % 2
            fns = []
            for ch in range(8):
                fns.append(lambda t, jj=j, cc=ch, bb=bank: t.matmul(
                    out=self.P[bb][:, 0:8],
                    lhsT=xn[:, cc * 2048 + jj * 128: cc * 2048 + (jj + 1) * 128],
                    rhs=wrbf[:, cc * 8:(cc + 1) * 8],
                    start=(cc == 0), stop=(cc == 7)))
            h_mm = p.batch("pe", fns, waits=[h_wrbf, h_xn, bank_reader.get(bank)])
            h1 = p.chain("dve", [
                lambda v, bb=bank: v.tensor_copy(out=lg[:, :], in_=self.P[bb][:, 0:8]),
                lambda v: v.reduce_max(out=rt[:, 0:1], in_=lg[:, :], axis=mybir.AxisListType.X),
                lambda v: v.tensor_scalar_mul(out=rt[:, 3:4], in0=rt[:, 0:1], scalar1=-1.0),
                lambda v: v.tensor_single_scalar(out=q8[:, :], in_=lg[:, :],
                                                 scalar=rt[:, 0:1], op=ALU.is_equal),
                lambda v: v.scalar_tensor_tensor(out=q8[:, :], in0=q8[:, :], scalar=-1e30,
                                                 in1=lg[:, :], op0=ALU.mult, op1=ALU.add),
                lambda v: v.reduce_max(out=rt[:, 1:2], in_=q8[:, :], axis=mybir.AxisListType.X),
            ], waits=[h_mm])
            bank_reader[bank] = h1
            h2 = p.batch("act", [
                lambda a: a.activation(out=rt[:, 4:5], in_=rt[:, 1:2], func=AF.Exp,
                                       bias=rt[:, 3:4], scale=1.0)], waits=[h1])
            h3 = p.batch("dve", [
                lambda v: v.tensor_scalar_add(out=rt[:, 5:6], in0=rt[:, 4:5], scalar1=1.0)],
                waits=[h2])
            h4 = p.batch("act", [
                lambda a: a.activation(out=q8[:, :], in_=lg[:, :], func=AF.Exp,
                                       bias=rt[:, 3:4], scale=1.0)], waits=[h3])
            h_prob[j] = p.batch("dve", [
                lambda v, jj=j: v.tensor_single_scalar(
                    out=self.prob_sb[:, jj * 8:(jj + 1) * 8], in_=q8[:, :],
                    scalar=rt[:, 5:6], op=ALU.is_ge)], waits=[h4])

        h_add = {}
        h_add_seq = {}
        h_y = {}
        for e in range(9):
            ep = e % 2
            if e == 0 and self.h_prew1 is not None:
                for q4 in range(2):
                    hc = p.batch("dve", [
                        lambda v, qq=q4: v.tensor_copy(
                            out=w1bf[:, qq * 2048:(qq + 1) * 2048],
                            in_=self.prew1[:, qq * 2048:(qq + 1) * 2048])],
                        waits=[self.h_prew1] + a_w1)
            else:
                for q4 in range(2):
                    sl = q4
                    base = (0 if e == 8 else e * 524288) + q4 * 4 * 65536
                    srcw = bass.AP(d["Ws1"] if e == 8 else d["W1"], base,
                                   [[512, 128], [65536, 4], [1, 512]])
                    dstw = bass.AP(wst, sl * 2048, [[4096, 128], [512, 4], [1, 512]])
                    hld = p.dma(f"ws{sl}", [lambda sp, sw=srcw, dw=dstw: sp.dma_start(
                        out=dw, in_=sw)],
                        waits=([stage_slot[sl]] if sl in stage_slot else a_xa))
                    hc = p.batch("dve", [
                        lambda v, ss=sl, pp=ep, qq=q4: v.tensor_copy(
                            out=w1bf[:, pp * 4096 + qq * 2048: pp * 4096 + (qq + 1) * 2048],
                            in_=wst[:, ss * 2048:(ss + 1) * 2048])],
                        waits=[hld, h_gate_last.get(e - 2)] + a_w1)
                    stage_slot[sl] = hc
            h_w1_ready = hc
            for q4 in range(2):
                sl = q4
                base = (0 if e == 8 else e * 524288) + q4 * 2 * 131072
                srcw = bass.AP(d["Ws2"] if e == 8 else d["W2"], base,
                               [[1024, 128], [131072, 2], [1, 1024]])
                dstw = bass.AP(wst, sl * 2048, [[4096, 128], [1024, 2], [1, 1024]])
                hld = p.dma(f"ws{sl}", [lambda sp, sw=srcw, dw=dstw: sp.dma_start(
                    out=dw, in_=sw)],
                    waits=([stage_slot[sl]] if sl in stage_slot else a_xa))
                hc = p.batch("dve", [
                    lambda v, ss=sl, pp=ep, qq=q4: v.tensor_copy(
                        out=w2bf[:, pp * 4096 + qq * 2048: pp * 4096 + (qq + 1) * 2048],
                        in_=wst[:, ss * 2048:(ss + 1) * 2048])],
                    waits=[hld, h_down_last.get(e - 2)] + a_w2)
                stage_slot[sl] = hc
            h_w2_ready = hc
            for tg in range(4):
                for fc in range(4):
                    bank = 2 + (fc % 2)
                    fns = []
                    for ch in range(8):
                        fns.append(lambda t, cc=ch, ff=fc, gg=tg, pp=ep, bb=bank: t.matmul(
                            out=self.P[bb][:, :],
                            lhsT=w1bf[:, pp * 4096 + cc * 512 + ff * 128:
                                      pp * 4096 + cc * 512 + (ff + 1) * 128],
                            rhs=xn[:, cc * 2048 + gg * 512: cc * 2048 + (gg + 1) * 512],
                            start=(cc == 0), stop=(cc == 7)))
                    hg = p.batch("pe", fns,
                                 waits=[h_w1_ready, h_xn, bank_reader.get(bank)])
                    if tg == 3 and fc == 3:
                        h_gate_last[e] = hg
                    hr = p.batch("act", [lambda a, bb=bank, ff=fc: a.activation(
                        out=rtile[:, (ff % 2) * 512:((ff % 2) + 1) * 512],
                        in_=self.P[bb][:, :], func=AF.Relu)],
                        waits=[hg, self.h_sq.get((e, tg, fc - 2)),
                               self.h_sq.get((e, tg - 1, fc + 2))])
                    bank_reader[bank] = hr
                    self.h_sq[(e, tg, fc)] = p.batch("dve", [
                        lambda v, ff=fc, gg=tg, pp=ep: v.tensor_tensor(
                            out=he[:, pp * 8192 + ff * 2048 + gg * 512:
                                    pp * 8192 + ff * 2048 + (gg + 1) * 512],
                            in0=rtile[:, (ff % 2) * 512:((ff % 2) + 1) * 512],
                            in1=rtile[:, (ff % 2) * 512:((ff % 2) + 1) * 512],
                            op=ALU.mult)],
                        waits=[hr, h_down_last.get(e - 2)])
                for jj in range(4):
                    j = tg * 4 + jj
                    for mo in range(2):
                        bank = 4 + ((j * 2 + mo) % 2)
                        fns = []
                        for fc in range(4):
                            fns.append(lambda t, cc=fc, jo=j, mm=mo, pp=ep, bb=bank: t.matmul(
                                out=self.P[bb][:, :],
                                lhsT=he[:, pp * 8192 + cc * 2048 + jo * 128:
                                         pp * 8192 + cc * 2048 + (jo + 1) * 128],
                                rhs=w2bf[:, pp * 4096 + cc * 1024 + mm * 512:
                                         pp * 4096 + cc * 1024 + (mm + 1) * 512],
                                start=(cc == 0), stop=(cc == 3)))
                        hdp = p.batch("pe", fns, waits=[
                            h_w2_ready,
                            self.h_sq[(e, tg, 0)], self.h_sq[(e, tg, 1)],
                            self.h_sq[(e, tg, 2)], self.h_sq[(e, tg, 3)],
                            bank_reader.get(bank)])
                        if j == 15 and mo == 1:
                            h_down_last[e] = hdp
                        yslot = (j * 2 + mo) % 2
                        gstep = e * 32 + j * 2 + mo
                        if e == 8:
                            h_add[(e, j, mo)] = p.batch("dve", [
                                lambda v, bb=bank, jo=j, mm=mo: v.tensor_tensor(
                                    out=self.x_tm[:, jo * 1024 + mm * 512:
                                                  jo * 1024 + (mm + 1) * 512],
                                    in0=self.x_tm[:, jo * 1024 + mm * 512:
                                                  jo * 1024 + (mm + 1) * 512],
                                    in1=self.P[bb][:, :], op=ALU.add)],
                                waits=[hdp, h_add.get((e - 1, j, mo)), h_xtm_readers,
                                       self.h_xtm_dma])
                        else:
                            h_add[(e, j, mo)] = p.batch("dve", [
                                lambda v, bb=bank, jo=j, mm=mo, ee=e: v.scalar_tensor_tensor(
                                    out=self.x_tm[:, jo * 1024 + mm * 512:
                                                  jo * 1024 + (mm + 1) * 512],
                                    in0=self.P[bb][:, :],
                                    scalar=self.prob_sb[:, jo * 8 + ee: jo * 8 + ee + 1],
                                    in1=self.x_tm[:, jo * 1024 + mm * 512:
                                                  jo * 1024 + (mm + 1) * 512],
                                    op0=ALU.mult, op1=ALU.add)],
                                waits=[hdp, h_prob[j], h_add.get((e - 1, j, mo)),
                                       h_xtm_readers, self.h_xtm_dma])
                        bank_reader[bank] = h_add[(e, j, mo)]
                        h_add_seq[gstep] = h_add[(e, j, mo)]
                        self.h_xtm = h_add[(e, j, mo)]
        self.h_xtm = p.batch("pool", [lambda g: g.engine_nop()],
                             waits=[h_add[(8, 15, 0)], h_add[(8, 15, 1)]])

        srcx = bass.AP(self.x_tm, 0, [[NOWN * D, 128], [D, 16], [1, D]])
        dstx = d["x_spill_d"][:, :].rearrange("(j p) d -> p j d", j=16, p=128)
        h_all_adds = [h_add[(8, j, mo)] for j in range(16) for mo in range(2)]
        self.h_spill = p.dma("spill", [lambda sp: sp.dma_start(out=dstx, in_=srcx)],
                             waits=h_all_adds)

    def phase_c(self):
        nc, p, sb, d = self.nc, self.p, self.sb, self.dram
        h_sp = getattr(self, "h_spill", None)
        fns = []
        for ch in range(8):
            src = d["xn_fm_d"][ch * 128:(ch + 1) * 128, :]
            fns.append(lambda sp, ss=src, cc=ch: sp.dma_start(
                out=self.xn_full[:, cc * S:(cc + 1) * S], in_=ss))
        h_xnf = p.dma("xnfull", fns,
                      waits=[h_sp] + (getattr(self, "h_a_xn_stores", None) or []))

        o = OFF_PH
        o_tm = sb.at("C_otm", [128, 16 * 1024], BF16, o); o += 32768
        self.o_tm = o_tm
        o = OFF_PH2
        wst = sb.at("C_wst", [128, 2 * 256], F32, o); o += 2048
        wq_bf = sb.at("C_wq", [128, 2 * 2048], BF16, o); o += 8192
        wk_bf = sb.at("C_wk", [128, 2 * 2048], BF16, o); o += 8192
        wv_bf = sb.at("C_wv", [128, 2 * 2048], BF16, o); o += 8192
        q_bf = sb.at("C_q", [128, 2 * 2048], BF16, o); o += 8192
        k_bf = sb.at("C_k", [128, 2 * 4096], BF16, o); o += 16384
        vT = sb.at("C_vT", [128, 32 * 256], BF16, o); o += 16384
        e_sb = sb.at("C_e", [128, 2 * 4 * 256], BF16, o); o += 4096
        av_sb = sb.at("C_av", [128, 2 * 256], BF16, o); o += 1024
        d1_sb = sb.at("C_d1", [64, 512], F32, o); o += 2048
        rd_sb = sb.at("C_rd", [128, 8], F32, o); o += 32
        tmp_sb = sb.at("C_tmp", [128, 64], F32, o); o += 256
        lvec = sb.at("C_lvec", [1, 256], F32, o); o += 1024
        lsc = sb.at("C_lsc", [1, 8], F32, o); o += 32
        onesrow = sb.at("C_onesrow", [1, 128], F32, o); o += 512
        assert o <= PBYTES, o

        h_ones = p.batch("dve", [
            lambda v: v.memset(onesrow[:, :], 1.0),
            lambda v: v.memset(lvec[:, 0:1], 1.0),
        ])
        onesf = lvec
        h_onesf = p.batch("dve", [lambda v: v.memset(self.st_c[:, 4:5], 1.0)])
        h1 = p.batch("dve", [
            lambda v: v.tensor_tensor(out=self.st_c[:, 5:6], in0=self.identf_sb[:, 128:129],
                                      in1=self.identf_sb[:, 129:130], op=ALU.mult),
        ], waits=[self.h_init])
        h4 = p.batch("pe", [
            lambda t: t.matmul(out=self.P[6][0:1, 0:1], lhsT=self.st_c[0:64, 5:6],
                               rhs=self.st_c[0:64, 4:5], start=True, stop=True,
                               skip_group_check=True),
            lambda t: t.matmul(out=self.P[6][0:1, 1:2], lhsT=self.st_c[64:128, 5:6],
                               rhs=self.st_c[64:128, 4:5], start=False, stop=True,
                               tile_position=(64, 0), skip_group_check=True),
        ], waits=[h1, h_onesf, self.h_pe_warm])
        h5 = p.batch("dve", [lambda v: v.tensor_copy(out=lsc[:, 0:2],
                                                     in_=self.P[6][0:1, 0:2])], waits=[h4])
        h6 = p.batch("act", [
            lambda a: a.activation(out=lsc[:, 2:4], in_=lsc[:, 0:2], func=AF.Exp),
        ], waits=[h5])
        h7 = p.chain("dve", [
            lambda v: v.tensor_tensor(out=lsc[:, 4:5], in0=lsc[:, 3:4],
                                      in1=lsc[:, 2:3], op=ALU.subtract),
            lambda v: v.tensor_scalar_add(out=lsc[:, 5:6], in0=lsc[:, 4:5], scalar1=-0.8),
        ], waits=[h6])
        h8 = p.batch("pe", [lambda t: t.matmul(
            out=self.P[6][0:128, 4:5], lhsT=onesrow[:, :], rhs=lsc[:, 5:6],
            start=True, stop=True, skip_group_check=True)], waits=[h7, h_ones, h5])
        h_neglam = p.batch("dve", [lambda v: v.tensor_copy(
            out=self.neglam_bc[:, :], in_=self.P[6][0:128, 4:5])], waits=[h8])
        h_p6_free = h_neglam

        bank_reader = {6: h_p6_free, 7: self.h_pe_warm}
        stage_slot = {}
        if "B" in self.phases:
            stage_slot[0] = self.h_xtm
            stage_slot[1] = self.h_xtm
        h_proj_last = {}
        h_wcast = {}
        h_attn_pe_last = {}
        masks = self.masks_sb

        def wload(hp, wname, ch, sl):
            src = bass.AP(d[wname], ch * 128 * 2048 + hp * 128,
                          [[2048, 128], [1024, 2], [1, 128]])
            return p.dma(f"cw{sl}", [lambda sp, ss=src, s2=sl: sp.dma_start(
                out=wst[:, s2 * 256:(s2 + 1) * 256], in_=ss)],
                waits=[stage_slot.get(sl)])

        for hp in range(C_NHP):
            par = hp % 2
            for wi, (wname, wbf) in enumerate((("Wq", wq_bf), ("Wk", wk_bf), ("Wv", wv_bf))):
                for ch in range(8):
                    sl = (wi * 8 + ch) % 2
                    hld = wload(hp, wname, ch, sl)
                    hc = p.batch("pool", [
                        lambda g, ss=sl, pp=par, cc=ch, wb=wbf: g.tensor_copy(
                            out=wb[:, pp * 2048 + cc * 256:(pp * 2048 + (cc + 1) * 256)],
                            in_=wst[:, ss * 256:(ss + 1) * 256])],
                        waits=[hld, h_proj_last.get(hp - 2)])
                    stage_slot[sl] = hc
            h_w_ready = hc

            h_pp = {}
            pj = 0
            for c in range(2):
                for blk in range(4):
                    bank = pj % 2
                    pj += 1
                    rhs = bass.AP(self.xn_full, 0, [[32768, 128], [0, 0], [0, 0]])
                    fns = []
                    for ch in range(8):
                        rr = bass.AP(self.xn_full, ch * S + blk * 1024,
                                     [[32768, 128], [256, 4], [1, 128]])
                        fns.append(lambda t, cc=ch, cp=c, pp=par, bb=bank, r2=rr: t.matmul(
                            out=self.P[bb][:, :],
                            lhsT=wq_bf[:, pp * 2048 + cc * 256 + cp * 128:
                                       pp * 2048 + cc * 256 + (cp + 1) * 128],
                            rhs=r2, start=(cc == 0), stop=(cc == 7)))
                    hm = p.batch("pe", fns, waits=[h_w_ready, h_xnf, bank_reader.get(bank)])
                    hv = p.batch("dve", [lambda v, bb=bank, cp=c, kk=blk: v.tensor_scalar_mul(
                        out=q_bf[:, cp * 2048 + kk * 512:(cp * 2048 + (kk + 1) * 512)],
                        in0=self.P[bb][:, :], scalar1=1.0)], waits=[hm])
                    bank_reader[bank] = hv
            for c in range(2):
                for blk in range(8):
                    bank = pj % 2
                    pj += 1
                    fns = []
                    for ch in range(8):
                        rr = self.xn_full[:, ch * S + blk * 512: ch * S + (blk + 1) * 512]
                        fns.append(lambda t, cc=ch, cp=c, pp=par, bb=bank, r2=rr: t.matmul(
                            out=self.P[bb][:, :],
                            lhsT=wk_bf[:, pp * 2048 + cc * 256 + cp * 128:
                                       pp * 2048 + cc * 256 + (cp + 1) * 128],
                            rhs=r2, start=(cc == 0), stop=(cc == 7)))
                    hm = p.batch("pe", fns, waits=[h_w_ready, h_xnf, bank_reader.get(bank)])
                    hv = p.batch("dve", [lambda v, bb=bank, cp=c, kk=blk: v.tensor_scalar_mul(
                        out=k_bf[:, cp * 4096 + kk * 512:(cp * 4096 + (kk + 1) * 512)],
                        in0=self.P[bb][:, :], scalar1=1.0)], waits=[hm])
                    bank_reader[bank] = hv
            for slot in range(32):
                bank = pj % 2
                pj += 1
                fns = []
                for ch in range(8):
                    fns.append(lambda t, cc=ch, ssl=slot, pp=par, bb=bank: t.matmul(
                        out=self.P[bb][:, 0:256],
                        lhsT=self.xn_full[:, cc * S + ssl * 128: cc * S + (ssl + 1) * 128],
                        rhs=wv_bf[:, pp * 2048 + cc * 256:(pp * 2048 + (cc + 1) * 256)],
                        start=(cc == 0), stop=(cc == 7)))
                hm = p.batch("pe", fns, waits=[h_w_ready, h_xnf, bank_reader.get(bank)])
                hv = p.batch("dve", [lambda v, bb=bank, ssl=slot: v.tensor_scalar_mul(
                    out=vT[:, ssl * 256:(ssl + 1) * 256],
                    in0=self.P[bb][:, 0:256], scalar1=0.2)], waits=[hm])
                bank_reader[bank] = hv
            h_proj_last[hp] = hv

            hta = h_proj_last[hp]
            self.h_otm_last = hta
            for i in range(C_NGRP if "noattn" not in C_CUT else 0):
                h_exp = {}
                h_avk = {}
                h_emask = {}
                h_eready = {}
                nk = 4 * i + 4

                def emit_av(k, p=p):
                    kp = k % 2
                    first = (k == 0)
                    afns = []
                    for c in range(2):
                        for hm in range(2):
                            afns.append(lambda t, cp=c, hh=hm, kk=k, pp2=kp, ff=first: t.matmul(
                                out=self.P[4 + cp][hh * 64:(hh + 1) * 64, 0:256],
                                lhsT=vT[:, kk * 256 + cp * 128 + hh * 64:
                                        kk * 256 + cp * 128 + (hh + 1) * 64],
                                rhs=e_sb[:, pp2 * 1024 + hh * 512 + cp * 256:
                                         pp2 * 1024 + hh * 512 + (cp + 1) * 256],
                                start=ff, stop=False,
                                tile_position=(0, hh * 64), skip_group_check=True))
                    for hm in range(2):
                        afns.append(lambda t, hh=hm, kk=k, pp2=kp, ff=first: t.matmul(
                            out=self.P[6][hh * 32:hh * 32 + 1, 0:512],
                            lhsT=self.ones_bf[:, :],
                            rhs=e_sb[:, pp2 * 1024 + hh * 512: pp2 * 1024 + (hh + 1) * 512],
                            start=ff, stop=False,
                            tile_position=(0, hh * 32), skip_group_check=True))
                    waits = [h_eready[k]]
                    if first:
                        waits += [bank_reader.get(4), bank_reader.get(5), bank_reader.get(6)]
                    return p.batch("pe", afns, waits=waits)

                for k in range(nk):
                    kp = k % 2
                    sb_bank = 2 * kp
                    fns = []
                    for c in range(2):
                        for hm in range(2):
                            fns.append(lambda t, cp=c, hh=hm, kk=k, ii=i, bb=sb_bank: t.matmul(
                                out=self.P[bb + hh][:, cp * 256:(cp + 1) * 256],
                                lhsT=k_bf[hh * 64:(hh + 1) * 64,
                                          cp * 4096 + kk * 128: cp * 4096 + (kk + 1) * 128],
                                rhs=q_bf[hh * 64:(hh + 1) * 64,
                                         cp * 2048 + ii * 256: cp * 2048 + (ii + 1) * 256],
                                start=(cp == 0), stop=True,
                                tile_position=(hh * 64, 0), skip_group_check=True))
                    if "nosc" in C_CUT:
                        fns = [lambda t: t.nop()]
                    hsc = p.batch("pe", fns, waits=[
                        h_proj_last[hp],
                        h_exp.get(k - 2), bank_reader.get(sb_bank),
                        bank_reader.get(sb_bank + 1)])
                    if k >= 1:
                        h_avk[k - 1] = emit_av(k - 1)
                    efns = [lambda a, pp2=kp: a.activation(
                        out=e_sb[:, pp2 * 1024: (pp2 + 1) * 1024],
                        in_=self.Pbig[pp2][:, :],
                        func=AF.Exp, scale=0.125)]
                    if "noexp" in C_CUT:
                        efns = [lambda a: a.nop()]
                    h_exp[k] = p.batch("act", efns, waits=[
                        hsc, h_avk.get(k - 2), h_emask.get(k - 2)])
                    bank_reader[sb_bank] = h_exp[k]
                    bank_reader[sb_bank + 1] = h_exp[k]
                    h_eready[k] = h_exp[k]
                    if k >= 4 * i and "nomask" not in C_CUT:
                        kk2 = k - 4 * i
                        mfns = []
                        for hm in range(2):
                            mfns.append(lambda g, pp2=kp, hh=hm, ii=i, k3=kk2: g.tensor_tensor(
                                out=bass.AP(e_sb, pp2 * 1024 + hh * 512,
                                            [[2048, 128], [256, 2], [1, 256]]),
                                in0=bass.AP(e_sb, pp2 * 1024 + hh * 512,
                                            [[2048, 128], [256, 2], [1, 256]]),
                                in1=bass.AP(self.masks_sb, (ii * 4 + k3) * 256,
                                            [[NGRP * 4 * 256, 128], [0, 2], [1, 256]]),
                                op=ALU.mult))
                        h_emask[k] = p.batch("dve", mfns, waits=[h_exp[k], self.h_init])
                        h_eready[k] = h_emask[k]
                h_avk[nk - 1] = emit_av(nk - 1)
                h_av = h_avk[nk - 1]
                h_av_ev = p.batch("act", [
                    lambda a: a.copy(out=av_sb[:, 0:256], in_=self.P[4][:, 0:256]),
                    lambda a: a.copy(out=av_sb[:, 256:512], in_=self.P[5][:, 0:256]),
                ], waits=[h_av])
                h_d_ev = p.batch("dve", [
                    lambda v: v.tensor_copy(out=d1_sb[0:1, :], in_=self.P[6][0:1, 0:512]),
                    lambda v: v.tensor_copy(out=d1_sb[32:33, :], in_=self.P[6][32:33, 0:512]),
                ], waits=[h_av])
                bank_reader[4] = h_av_ev
                bank_reader[5] = h_av_ev
                bank_reader[6] = h_d_ev
                for q2 in range(2):
                    j = 2 * i + q2
                    htd = p.batch("pe", [
                        lambda t, qq=q2: t.transpose(
                            out=self.P[0][:, 0:64],
                            in_=d1_sb[:, qq * 128:(qq + 1) * 128],
                            identity=self.identf_sb[0:64, 0:64]),
                        lambda t, qq=q2: t.transpose(
                            out=self.P[1][:, 0:64],
                            in_=d1_sb[:, 256 + qq * 128: 256 + (qq + 1) * 128],
                            identity=self.identf_sb[0:64, 0:64]),
                    ], waits=[h_d_ev, bank_reader.get(0), bank_reader.get(1)])
                    hrd = p.chain("dve", [
                        lambda v: v.reciprocal(
                            out=rd_sb[:, 0:2],
                            in_=bass.AP(self.P01, 0, [[1024, 128], [32, 2]])),
                        lambda v: v.reciprocal(
                            out=rd_sb[:, 2:4],
                            in_=bass.AP(self.P01, 512, [[1024, 128], [32, 2]])),
                        lambda v: v.tensor_scalar_mul(
                            out=rd_sb[:, 4:6], in0=rd_sb[:, 2:4],
                            scalar1=self.neglam_bc[:, :]),
                    ], waits=[htd, h_neglam])
                    bank_reader[0] = hrd
                    bank_reader[1] = hrd
                    hta = p.batch("pe", [
                        lambda t, qq=q2: t.transpose(
                            out=self.pbf16(2), in_=av_sb[:, qq * 128:(qq + 1) * 128],
                            identity=self.ident_sb[:, :]),
                        lambda t, qq=q2: t.transpose(
                            out=self.pbf16(3), in_=av_sb[:, 256 + qq * 128: 256 + (qq + 1) * 128],
                            identity=self.ident_sb[:, :]),
                    ], waits=[h_av_ev, bank_reader.get(2), bank_reader.get(3), self.h_init])
                    hdiv = None
                    for hm in range(2):
                        hdiv = p.chain("dve", [
                            lambda v, hh=hm: v.tensor_scalar_mul(
                                out=tmp_sb[:, :],
                                in0=self.pbf16(2)[:, hh * 64:(hh + 1) * 64],
                                scalar1=rd_sb[:, hh:hh + 1]),
                            lambda v, hh=hm, jj=j, hp2=hp: v.scalar_tensor_tensor(
                                out=o_tm[:, jj * 1024 + hp2 * 128 + hh * 64:
                                         jj * 1024 + hp2 * 128 + (hh + 1) * 64],
                                in0=self.pbf16(3)[:, hh * 64:(hh + 1) * 64],
                                scalar=rd_sb[:, 4 + hh:5 + hh],
                                in1=tmp_sb[:, :], op0=ALU.mult, op1=ALU.add),
                        ], waits=[hta, hrd, hdiv])
                    bank_reader[2] = hdiv
                    bank_reader[3] = hdiv
                    self.h_otm_last = hdiv
            h_attn_pe_last[hp] = h_av if False else hta
        self.h_c_done = [self.h_otm_last, h_attn_pe_last[C_NHP - 1]]
        if "W" not in self.phases:
            srco = bass.AP(o_tm, 0, [[16384, 128], [1024, 16], [1, 1024]])
            dsto = d["o_tm_d"][:, :].rearrange("(j p) e -> p j e", j=16, p=128)
            p.dma("osave", [lambda sp: sp.dma_start(out=dsto, in_=srco)],
                  waits=[self.h_otm_last])

    def phase_w(self):
        nc, p, sb, d = self.nc, self.p, self.sb, self.dram
        o = OFF_PH2
        o_fm = sb.at("W_ofm", [128, 8 * 2048], BF16, o); o += 32768
        wst = sb.at("W_wst", [128, 2 * 512], F32, o); o += 4096
        wo_bf = sb.at("W_wo", [128, 8 * 1024], BF16, o); o += 16384
        at_sb = sb.at("W_at", [128, 2 * 512], BF16, o); o += 2048
        assert o <= PBYTES, o

        h_cdone = getattr(self, "h_c_done", [])
        if "C" in self.phases:
            o_tm = self.o_tm
            h_osrc = self.h_otm_last
        else:
            o_tm = sb.at("C_otm", [128, 16 * 1024], BF16, OFF_PH)
            srco = d["o_tm_d"][:, :].rearrange("(j p) e -> p j e", j=16, p=128)
            dsto = bass.AP(o_tm, 0, [[16384, 128], [1024, 16], [1, 1024]])
            h_osrc = p.dma("oload", [lambda sp: sp.dma_start(out=dsto, in_=srco)])

        bank_reader = {}
        stage_slot = {}
        if "C" in self.phases:
            for _sl in (0, 1):
                stage_slot[_sl] = self.h_otm_last
        h_wc0 = {}
        for ch in range(8):
            for half in range(2):
                sl = half
                src = d["Wo"][:, :].rearrange("(c m) o -> m c o", c=8, m=128)[
                    :, ch, half * 512:(half + 1) * 512]
                hld = p.dma(f"wo{sl}", [lambda sp, ss=src, s2=sl: sp.dma_start(
                    out=wst[:, s2 * 512:(s2 + 1) * 512], in_=ss)],
                    waits=[stage_slot.get(sl)])
                hc = p.batch("dve", [lambda v, s2=sl, cc=ch, hh=half: v.tensor_copy(
                    out=wo_bf[:, cc * 1024 + hh * 512: cc * 1024 + (hh + 1) * 512],
                    in_=wst[:, s2 * 512:(s2 + 1) * 512])], waits=[hld])
                stage_slot[sl] = hc
        h_wo_ready = hc
        srcx = d["x_spill_d"][:, :].rearrange("(j p) d -> p j d", j=16, p=128)
        dstx = bass.AP(self.x_tm, 0, [[NOWN * D, 128], [D, 16], [1, D]])
        h_xl = p.dma("xload2", [lambda sp: sp.dma_start(out=dstx, in_=srcx)],
                     waits=h_cdone)
        self.h_xtm_dma = h_xl
        self.h_xtm = h_xl
        h_ofm = {}
        for j in range(16):
            for ch in range(8):
                bank = (j * 8 + ch) % 4
                hpe = p.batch("pe", [lambda t, jj=j, cc=ch, bb=bank: t.transpose(
                    out=self.pbf16(bb),
                    in_=o_tm[:, jj * 1024 + cc * 128: jj * 1024 + (cc + 1) * 128],
                    identity=self.ident_sb[:, :])],
                    waits=[h_osrc, self.h_init, bank_reader.get(bank)])
                hev = p.batch("act", [lambda a, jj=j, cc=ch, bb=bank: a.copy(
                    out=o_fm[:, cc * 2048 + jj * 128: cc * 2048 + (jj + 1) * 128],
                    in_=self.pbf16(bb))], waits=[hpe])
                bank_reader[bank] = hev
        h_ofm_done = hev
        h_add = {}
        at_reader = {}
        for mo in range(8):
            for tg in range(4):
                bank = 4 + ((mo * 4 + tg) % 2)
                fns = []
                for ch in range(8):
                    fns.append(lambda t, cc=ch, mm=mo, gg=tg, bb=bank: t.matmul(
                        out=self.P[bb][:, :],
                        lhsT=wo_bf[:, cc * 1024 + mm * 128:(cc * 1024 + (mm + 1) * 128)],
                        rhs=o_fm[:, cc * 2048 + gg * 512: cc * 2048 + (gg + 1) * 512],
                        start=(cc == 0), stop=(cc == 7)))
                hmm = p.batch("pe", fns, waits=[h_wo_ready, h_ofm_done,
                                                bank_reader.get(bank)])
                ys = (mo * 4 + tg) % 2
                hev = p.batch("act", [lambda a, bb=bank, y2=ys: a.copy(
                    out=at_sb[:, y2 * 512:(y2 + 1) * 512], in_=self.P[bb][:, :])],
                    waits=[hmm, at_reader.get(ys)])
                bank_reader[bank] = hev
                for tt in range(4):
                    j = tg * 4 + tt
                    bank2 = (tt % 2)
                    hpe = p.batch("pe", [lambda t, y2=ys, t3=tt, bb=bank2: t.transpose(
                        out=self.pbf16(bb),
                        in_=at_sb[:, y2 * 512 + t3 * 128: y2 * 512 + (t3 + 1) * 128],
                        identity=self.ident_sb[:, :])],
                        waits=[hev, bank_reader.get(bank2)])
                    at_reader[ys] = hpe
                    h_add[(mo, tg, tt)] = p.batch("dve", [
                        lambda v, jj=j, mm=mo, bb=bank2: v.tensor_tensor(
                            out=self.x_tm[:, jj * 1024 + mm * 128: jj * 1024 + (mm + 1) * 128],
                            in0=self.x_tm[:, jj * 1024 + mm * 128: jj * 1024 + (mm + 1) * 128],
                            in1=self.pbf16(bb), op=ALU.add)],
                        waits=[hpe, self.h_xtm_dma, h_add.get((mo - 1, tg, tt))])
                    bank_reader[bank2] = h_add[(mo, tg, tt)]
        self.h_w_tile = {tg2 * 4 + tt2: h_add[(7, tg2, tt2)]
                         for tg2 in range(4) for tt2 in range(4)}
        self.h_xtm = p.batch("dve", [lambda v: v.engine_nop()],
                             waits=[h_add[(7, tg2, tt2)] for tg2 in range(4) for tt2 in range(4)])
        if "E" not in self.phases and "G" not in self.phases:
            srcx2 = bass.AP(self.x_tm, 0, [[NOWN * D, 128], [D, 16], [1, D]])
            dstx2 = d["x_mid_d"][:, :].rearrange("(j p) d -> p j d", j=16, p=128)
            p.dma("xmid", [lambda sp: sp.dma_start(out=dstx2, in_=srcx2)],
                  waits=[self.h_xtm])

    def phase_d(self):
        nc, p, sb, d = self.nc, self.p, self.sb, self.dram
        self.xn1_own_fm = sb.at("xn1_own_fm", [128, 8 * 2048], BF16, OFF_PH)
        o = OFF_PH2 + 60416
        xc = sb.at("D_xc", [128, 2 * D], F32, o); o += 2 * 4096
        sqd = sb.at("D_sqd", [128, D], BF16, o); o += 2048
        xnt = sb.at("D_xnt", [128, 2 * D], BF16, o); o += 2 * 2048
        assert o <= PBYTES, o
        st_a, st_b, st_c = self.st_a, self.st_b, self.st_c
        h_stats = {}
        h_act1 = {}
        h_xn = {}
        h_pe = {}
        h_evac = {}
        h_w_tile = getattr(self, "h_w_tile", {})
        for j in range(16):
            par = j % 2
            h_x = h_w_tile.get(j, self.h_xtm)
            src = self.x_tm[:, j * D:(j + 1) * D]
            h_sqd2 = p.batch("act", [
                lambda a, srcc=src, pp=par: a.activation(
                    out=sqd[:, :], in_=srcc, func=AF.Square,
                    accum_out=st_b[:, pp:pp + 1]),
            ], waits=[h_x, h_xn.get(j - 2), self.h_ones])
            h_stats[j] = p.chain("dve", [
                lambda v, srcc=src, pp=par: v.reduce_sum(
                    out=st_a[:, 4 + pp:5 + pp], in_=srcc, axis=mybir.AxisListType.X),
                lambda v, pp=par: v.tensor_scalar_mul(
                    out=st_a[:, pp:pp + 1], in0=st_a[:, 4 + pp:5 + pp], scalar1=1.0 / D),
                lambda v, pp=par: v.scalar_tensor_tensor(
                    out=st_a[:, 2 + pp:3 + pp], in0=st_a[:, pp:pp + 1],
                    scalar=-1.0, in1=st_a[:, pp:pp + 1], op0=ALU.mult, op1=ALU.mult),
                lambda v, pp=par: v.tensor_scalar_add(
                    out=st_a[:, 6 + pp:7 + pp], in0=st_a[:, 2 + pp:3 + pp], scalar1=1e-5),
            ], waits=[h_x, h_xn.get(j - 2)])
            h_act1[j] = p.batch("act", [
                lambda a, pp=par: a.activation(
                    out=st_b[:, 4 + pp:5 + pp], in_=st_b[:, pp:pp + 1], func=AF.Sqrt,
                    scale=1.0 / D, bias=st_a[:, 6 + pp:7 + pp]),
            ], waits=[h_stats[j], h_sqd2])
            h_xn[j] = p.chain("dve", [
                lambda v, pp=par: v.reciprocal(
                    out=st_c[:, pp:pp + 1], in_=st_b[:, 4 + pp:5 + pp]),
                lambda v, pp=par: v.scalar_tensor_tensor(
                    out=st_c[:, 2 + pp:3 + pp], in0=st_a[:, pp:pp + 1],
                    scalar=-1.0, in1=st_c[:, pp:pp + 1], op0=ALU.mult, op1=ALU.mult),
                lambda v, srcc=src, pp=par: v.tensor_scalar(
                    out=xnt[:, pp * D:(pp + 1) * D], in0=srcc,
                    scalar1=st_c[:, pp:pp + 1], scalar2=st_c[:, 2 + pp:3 + pp],
                    op0=ALU.mult, op1=ALU.add),
            ], waits=[h_act1[j]])
            for ch in range(8):
                bank = ch % 4
                h_pe[(j, ch)] = p.batch("pe", [
                    lambda t, pp=par, cc=ch, bb=bank: t.transpose(
                        out=self.pbf16(bb),
                        in_=xnt[:, pp * D + cc * 128: pp * D + (cc + 1) * 128],
                        identity=self.ident_sb[:, :]),
                ], waits=[h_xn[j], self.h_init, self.h_pbank.get(bank)])
                dst = self.xn1_own_fm[:, ch * 2048 + j * 128: ch * 2048 + (j + 1) * 128]
                h_evac[(j, ch)] = p.batch("act", [
                    lambda a, dd2=dst, bb=bank: a.copy(out=dd2, in_=self.pbf16(bb)),
                ], waits=[h_pe[(j, ch)]])
                self.h_pbank[bank] = h_evac[(j, ch)]
            srcsb = bass.AP(self.xn1_own_fm, j * 128, [[16384, 128], [2048, 8], [1, 128]])
            dstd = bass.AP(d["xn1_fm_d"], j * 128, [[2048, 128], [128 * 2048, 8], [1, 128]])
            p.dma("d1st", [lambda sp, ss=srcsb, dd2=dstd: sp.dma_start(out=dd2, in_=ss)],
                  waits=[h_evac[(j, 7)]])
        self.h_d_tile = {j2: h_evac[(j2, 7)] for j2 in range(16)}
        self.h_xn1_ready = h_evac[(15, 7)]

    def phase_e(self):
        nc, p, sb, d = self.nc, self.p, self.sb, self.dram
        if "D" not in self.phases:
            self.load_x_tm("x_mid_d")
            self.xn1_own_fm = sb.at("xn1_own_fm", [128, 8 * 2048], BF16, OFF_PH)
            fns = []
            for ch in range(8):
                srcap = bass.AP(d["xn1_fm_d"], ch * 128 * 2048, [[2048, 128], [1, 2048]])
                dstap = self.xn1_own_fm[:, ch * 2048:(ch + 1) * 2048]
                fns.append(lambda sp, ss=srcap, dd2=dstap: sp.dma_start(out=dd2, in_=ss))
            self.h_xn1_ready = p.dma("xn1load", fns)
        h_xn1 = self.h_xn1_ready
        xn1 = self.xn1_own_fm
        o = OFF_PH2
        wst = sb.at("E_wst", [128, 2 * 512], F32, o); o += 4096
        wbf = sb.at("E_wbf", [128, 8 * 1024], BF16, o); o += 16384
        ktmp = sb.at("E_ktmp", [128, D], F32, o); o += 4096
        xs = sb.at("E_xs", [128, D], F32, o); o += 4096
        sms = sb.at("E_sms", [128, 192], F32, o); o += 768
        out_tm = sb.at("E_outtm", [128, 2 * D], BF16, o); o += 4096
        grp_q = sb.at("E_gq", [128, 128 * 64], BF16, o); o += 16384
        grp_k = sb.at("E_gk", [128, 128 * 64], BF16, o); o += 16384
        grp_v = sb.at("E_gv", [128, 128 * 64], BF16, o); o += 16384
        tds = sb.at("E_tds", [64, 2 * 1024], BF16, o); o += 4096
        msb = sb.at("E_msb", [128, 2 * 64], BF16, OFF_PH2 + 20480)
        mab = sb.at("E_mab", [128, 512], F32, OFF_PH2 + 20992)
        mblk = sb.at("E_mblk", [128, 512], BF16, OFF_PH2 + 23040)
        a1g = sb.at("E_a1g", [128, 2 * 512], BF16, o); o += 2048
        a1t = sb.at("E_a1t", [128, 2 * D], BF16, OFF_PH2)
        assert o <= PBYTES, o

        slot_sem = {}
        if "D" in self.phases:
            slot_sem[0] = self.h_xn1_ready
            slot_sem[1] = self.h_xn1_ready
        h_store = {}
        h_outuser = {}
        for wi, wname in enumerate(("WVT", "WKT", "WQT")):
            h_wc = {}
            for ch in range(8):
                for nb in range(2):
                    sl = (ch * 2 + nb) % 2
                    src = d[wname][ch * 128:(ch + 1) * 128, nb * 512:(nb + 1) * 512]
                    hld = p.dma(f"ew{sl}", [lambda sp, ss=src, s2=sl: sp.dma_start(
                        out=wst[:, s2 * 512:(s2 + 1) * 512], in_=ss)],
                        waits=[slot_sem.get(sl)])
                    h_wc[(ch, nb)] = p.batch("pool", [
                        lambda g, s2=sl, cc=ch, nn=nb: g.tensor_copy(
                            out=wbf[:, cc * 1024 + nn * 512: cc * 1024 + (nn + 1) * 512],
                            in_=wst[:, s2 * 512:(s2 + 1) * 512])],
                        waits=[hld, self.h_eproj_last.get((wi - 1, 15))])
                    slot_sem[sl] = h_wc[(ch, nb)]
            h_wrdy = h_wc[(7, 1)]
            for j in range(16):
                par = j % 2
                h_ev = {}
                for nb in range(2):
                    bank = nb
                    fns = []
                    for ch in range(8):
                        fns.append(lambda t, cc=ch, jj=j, nn=nb, bb=bank: t.matmul(
                            out=self.P[bb][:, :],
                            lhsT=xn1[:, cc * 2048 + jj * 128: cc * 2048 + (jj + 1) * 128],
                            rhs=wbf[:, cc * 1024 + nn * 512: cc * 1024 + (nn + 1) * 512],
                            start=(cc == 0), stop=(cc == 7)))
                    base_w = [getattr(self, "h_d_tile", {}).get(j, h_xn1),
                              self.h_pbank.get(bank)]
                    if j == 0:
                        for cc2 in range(8):
                            hmm2 = p.batch("pe", [fns[cc2]], waits=(
                                (base_w + [h_wc[(cc2, nb)]]) if cc2 == 0
                                else [h_wc[(cc2, nb)]]))
                    else:
                        hmm2 = p.batch("pe", fns, waits=[h_wrdy] + base_w)
                    self.h_eproj_last[(wi, j)] = hmm2
                    if wi == 0:
                        hv = p.batch("dve", [lambda v, bb=bank, pp=par, nn=nb: v.tensor_copy(
                            out=out_tm[:, pp * D + nn * 512: pp * D + (nn + 1) * 512],
                            in_=self.P[bb][:, :])],
                            waits=[hmm2, h_outuser.get(par)])
                    else:
                        hv = p.batch("dve", [lambda v, bb=bank, nn=nb: v.tensor_copy(
                            out=ktmp[:, nn * 512:(nn + 1) * 512],
                            in_=self.P[bb][:, :])],
                            waits=[hmm2, self.h_sm.get((wi, j - 1))])
                    self.h_pbank[bank] = hv
                    h_ev[nb] = hv
                if wi == 0:
                    srcv = out_tm[:, par * D:(par + 1) * D]
                    dstv = d["vv_d"][j * 128:(j + 1) * 128, :]
                    h_store[(wi, j)] = p.dma(
                        f"es{par}", [lambda sp, ss=srcv, dd2=dstv: sp.dma_start(
                            out=dd2, in_=ss)],
                        waits=[h_ev[0], h_ev[1]])
                    h_outuser[par] = h_store[(wi, j)]
                else:
                    if wi == 1:
                        rd_in = bass.AP(ktmp, 0, [[1024, 128], [1, 64], [64, 16]])
                        mxc = 64
                        b0 = bass.AP(sms, 0, [[192, 128], [0, 16], [1, 64]])
                        b1 = bass.AP(sms, 64, [[192, 128], [0, 16], [1, 64]])
                    else:
                        rd_in = bass.AP(ktmp, 0, [[1024, 128], [64, 16], [1, 64]])
                        mxc = 16
                        b0 = bass.AP(sms, 0, [[192, 128], [1, 16], [0, 64]])
                        b1 = bass.AP(sms, 64, [[192, 128], [1, 16], [0, 64]])
                    o3 = bass.AP(xs, 0, [[1024, 128], [64, 16], [1, 64]])
                    i3 = bass.AP(ktmp, 0, [[1024, 128], [64, 16], [1, 64]])
                    hsm1 = p.chain("dve", [
                        lambda v, r2=rd_in, mc=mxc: v.reduce_max(
                            out=sms[:, 0:mc], in_=r2, axis=mybir.AxisListType.X),
                        lambda v, mc=mxc: v.tensor_scalar_mul(
                            out=sms[:, 0:mc], in0=sms[:, 0:mc], scalar1=-1.0),
                        lambda v, o2=o3, i2=i3, bb0=b0: v.tensor_tensor(
                            out=o2, in0=i2, in1=bb0, op=ALU.add),
                    ], waits=[h_ev[0], h_ev[1]])
                    hsm2 = p.batch("act", [
                        lambda a: a.activation(out=xs[:, :], in_=xs[:, :],
                                               func=AF.Exp)], waits=[hsm1])
                    hsm3 = p.chain("dve", [
                        lambda v, r2=rd_in, mc=mxc: v.reduce_sum(
                            out=sms[:, 128:128 + mc],
                            in_=bass.AP(xs, 0, r2.ap), axis=mybir.AxisListType.X),
                        lambda v, mc=mxc: v.reciprocal(
                            out=sms[:, 64:64 + mc], in_=sms[:, 128:128 + mc]),
                        lambda v, i2=i3, bb1=b1, pp=par: v.tensor_tensor(
                            out=bass.AP(out_tm, pp * D, [[2 * D, 128], [64, 16], [1, 64]]),
                            in0=bass.AP(xs, 0, i2.ap),
                            in1=bb1, op=ALU.mult),
                    ], waits=[hsm2, h_outuser.get(par)])
                    self.h_sm[(wi, j)] = hsm3
                    srcv = out_tm[:, par * D:(par + 1) * D]
                    dstv = d[("sk_d" if wi == 1 else "sq_d")][j * 128:(j + 1) * 128, :]
                    h_store[(wi, j)] = p.dma(
                        f"es{par}", [lambda sp, ss=srcv, dd2=dstv: sp.dma_start(
                            out=dd2, in_=ss)],
                        waits=[hsm3])
                    h_outuser[par] = h_store[(wi, j)]
        h_qkv_done = [h_store[(wi, j)] for wi in range(3) for j in (14, 15)]

        bank_reader = dict(self.h_pbank)
        h_a1_store = {}
        h_ev4 = {}
        h_mext = {}
        h_mblk4 = {}
        h_a1ev = {}
        for half in range(2):
            fns = []
            for ti, tname in enumerate(("sq_d", "sk_d", "vv_d")):
                src = bass.AP(d[tname], half * 128 * 8192, [[64, 128], [8192, 128], [1, 64]])
                dsts = (grp_q, grp_k, grp_v)[ti]
                fns.append(lambda sp, ss=src, dd2=dsts: sp.dma_start(
                    out=dd2[:, :], in_=ss))
            h_bnc = p.dma(f"bn{half}", fns,
                          waits=list(h_qkv_done) + [self.h_mmb.get(half * 32 - 1)])
            for B4 in range(32):
                bp = B4 % 2
                g0 = B4 * 4
                gbat = half * 32 + B4
                tf = []
                for g4 in range(4):
                    tf.append(lambda t, gg=g0 + g4, pp=bp, g4b=g4: t.transpose(
                        out=self.pbf16(2 + pp, rows=64, cols=128, col0=g4b * 128),
                        in_=grp_q[:, gg * 64:(gg + 1) * 64],
                        identity=self.ident_sb[:, :]))
                    tf.append(lambda t, gg=g0 + g4, pp=bp, g4b=g4: t.transpose(
                        out=self.pbf16(2 + pp, rows=64, cols=128, col0=512 + g4b * 128),
                        in_=grp_k[:, gg * 64:(gg + 1) * 64],
                        identity=self.ident_sb[:, :]))
                htr = p.batch("pe", tf, waits=[h_bnc, h_ev4.get(gbat - 2), self.h_init])
                h_ev4[gbat] = p.batch("act", [
                    lambda a, pp=bp: a.copy(out=tds[:, pp * 1024:(pp + 1) * 1024],
                                            in_=self.pbf16(2 + pp, rows=64, cols=1024))],
                    waits=[htr, h_mext.get(gbat - 2)])
                maf = []
                for g4 in range(4):
                    maf.append(lambda t, pp=bp, g4b=g4: t.matmul(
                        out=self.P[4 + pp][:, g4b * 128:(g4b + 1) * 128],
                        lhsT=tds[:, pp * 1024 + 512 + g4b * 128: pp * 1024 + 512 + (g4b + 1) * 128],
                        rhs=tds[:, pp * 1024 + g4b * 128: pp * 1024 + (g4b + 1) * 128],
                        start=(g4b == 0), stop=True, skip_group_check=True))
                hma = p.batch("pe", maf, waits=[h_ev4[gbat], h_mext.get(gbat - 2)])
                h_mext[gbat] = p.chain("dve", [
                    lambda v, pp=bp: v.tensor_tensor(
                        out=mab[:, :], in0=self.P[4 + pp][:, 0:512],
                        in1=bass.AP(self.diagmask_sb, 0, [[128, 128], [0, 4], [1, 128]]),
                        op=ALU.mult),
                    lambda v, pp=bp: self._lowp(lambda: v.reduce_sum(
                        out=msb[:, pp * 64:(pp + 1) * 64],
                        in_=bass.AP(mab, 0, [[512, 128], [128, 4], [1, 16], [16, 8]]),
                        axis=mybir.AxisListType.X)),
                ], waits=[hma, h_mblk4.get(gbat - 2)])
                h_mblk4[gbat] = p.batch("dve", [lambda g2, pp=bp: g2.tensor_tensor(
                    out=mblk[:, :],
                    in0=bass.AP(msb, pp * 64, [[128, 128], [16, 4], [0, 8], [1, 16]]),
                    in1=bass.AP(self.diagmask_sb, 0, [[128, 128], [0, 4], [1, 128]]),
                    op=ALU.mult)],
                    waits=[h_mext[gbat], self.h_mmb.get(gbat - 1)])
                mbf = []
                for g4 in range(4):
                    mbf.append(lambda t, pp=bp, g4b=g4, gg=g0 + g4: t.matmul(
                        out=self.P[6 + pp][:, g4b * 64:(g4b + 1) * 64],
                        lhsT=mblk[:, g4b * 128:(g4b + 1) * 128],
                        rhs=grp_v[:, gg * 64:(gg + 1) * 64],
                        start=(g4b == 0), stop=True, skip_group_check=True))
                hb = p.batch("pe", mbf, waits=[h_mblk4[gbat], h_a1ev.get(gbat - 2)])
                self.h_mmb[gbat] = hb
                sl8 = (gbat // 2) % 2
                h_a1ev[gbat] = p.batch("act", [lambda a, pp=bp, s8=sl8, bb=B4: a.copy(
                    out=a1g[:, s8 * 512 + (bb % 2) * 256: s8 * 512 + (bb % 2 + 1) * 256],
                    in_=self.P[6 + pp][:, 0:256])],
                    waits=[hb, h_a1_store.get(gbat // 2 - 2)])
                if B4 % 2 == 1:
                    srca = bass.AP(a1g, sl8 * 512, [[1024, 128], [64, 8], [1, 64]])
                    dsta = bass.AP(d["a1_d"], (g0 - 4 + half * 128) * 8 * 1024,
                                   [[64, 128], [8192, 8], [1, 64]])
                    h_a1_store[gbat // 2] = p.dma(
                        f"ea{sl8}", [lambda sp, ss=srca, dd2=dsta: sp.dma_start(
                            out=dd2, in_=ss)], waits=[h_a1ev[gbat], h_a1ev[gbat - 1]])
        h_a1_all = [h_a1_store[z] for z in range(32)]

        h_add = {}
        for j in range(16):
            par = j % 2
            src = d["a1_d"][j * 128:(j + 1) * 128, :]
            hld = p.dma(f"ar{par}", [lambda sp, ss=src, pp=par: sp.dma_start(
                out=a1t[:, pp * D:(pp + 1) * D], in_=ss)],
                waits=h_a1_all + ([h_add.get(j - 2)] if j >= 2 else []))
            h_add[j] = p.batch("dve", [lambda v, jj=j, pp=par: v.tensor_tensor(
                out=self.x_tm[:, jj * D:(jj + 1) * D],
                in0=self.x_tm[:, jj * D:(jj + 1) * D],
                in1=a1t[:, pp * D:(pp + 1) * D], op=ALU.add)],
                waits=[hld, self.h_xtm, self.h_xtm_dma])
        self.h_xtm = p.batch("dve", [lambda v: v.nop()],
                             waits=[h_add[j] for j in range(16)])
        if "F" not in self.phases:
            srcx2 = bass.AP(self.x_tm, 0, [[NOWN * D, 128], [D, 16], [1, D]])
            dstx2 = d["x_l1_d"][:, :].rearrange("(j p) d -> p j d", j=16, p=128)
            p.dma("xl1", [lambda sp: sp.dma_start(out=dstx2, in_=srcx2)],
                  waits=[self.h_xtm])

    def phase_f(self):
        nc, p, sb, d = self.nc, self.p, self.sb, self.dram
        if "E" not in self.phases:
            self.load_x_tm("x_l1_d")
            self.xn1_own_fm = sb.at("xn1_own_fm", [128, 8 * 2048], BF16, OFF_PH)
            fns = []
            for ch in range(8):
                srcap = bass.AP(d["xn1_fm_d"], ch * 128 * 2048, [[2048, 128], [1, 2048]])
                dstap = self.xn1_own_fm[:, ch * 2048:(ch + 1) * 2048]
                fns.append(lambda sp, ss=srcap, dd2=dstap: sp.dma_start(out=dd2, in_=ss))
            self.h_xn1_ready = p.dma("xn1load2", fns)
        h_xn1 = self.h_xn1_ready
        xn1 = self.xn1_own_fm
        o = OFF_PH2
        fst = sb.at("F_fst", [128, 2 * 1024], I32, o); o += 2 * 4096
        wgq = sb.at("F_wgq", [128, 8 * 768], BF16, o); o += 12288
        wuq = sb.at("F_wuq", [128, 8 * 768], BF16, o); o += 12288
        wdq = sb.at("F_wdq", [128, 6 * 1024], BF16, o); o += 12288
        hq = sb.at("F_hq", [128, 6 * 2048], BF16, o); o += 24576
        sil = sb.at("F_sil", [128, 2 * 512], BF16, o); o += 2048
        assert o <= PBYTES, o

        slot_sem = {}
        if "E" in self.phases:
            slot_sem[0] = self.h_xtm
            slot_sem[1] = self.h_xtm
        h_gu_pe = {}
        h_hq = {}
        h_dn_pe = {}
        h_add = {}
        h_xadd = self.h_xtm
        for fq in range(4):
            h_cast_g = {}
            for wi, (wname, wq) in enumerate((("WGT", wgq), ("WUT", wuq))):
                for ch in range(8):
                    sl = ch % 2
                    src = d[wname][ch * 128:(ch + 1) * 128, fq * 768:(fq + 1) * 768]
                    hld = p.dma(f"fw{sl}", [lambda sp, ss=src, s2=sl: sp.dma_start(
                        out=fst[:, s2 * 1024:s2 * 1024 + 768], in_=ss)],
                        waits=[slot_sem.get(sl)])
                    h_cast_g[(wi, ch)] = p.batch("pool", [
                        lambda g, s2=sl, cc=ch, wq2=wq: g.tensor_copy(
                            out=wq2[:, cc * 768:(cc + 1) * 768],
                            in_=fst[:, s2 * 1024:s2 * 1024 + 768])],
                        waits=[hld, h_gu_pe.get((fq - 1, wi, 9))])
                    slot_sem[sl] = h_cast_g[(wi, ch)]
            h_gu_rdy = h_cast_g[(1, 7)]
            for fc in range(6):
                sl = fc % 2
                src = bass.AP(d["WDT"], (fq * 768 + fc * 128) * 1024, [[1024, 128], [1, 1024]])
                hld = p.dma(f"fw{sl}", [lambda sp, ss=src, s2=sl: sp.dma_start(
                    out=fst[:, s2 * 1024:(s2 + 1) * 1024], in_=ss)],
                    waits=[slot_sem.get(sl)])
                hcd = p.batch("pool", [
                    lambda g, s2=sl, cc=fc: g.tensor_copy(
                        out=wdq[:, cc * 1024:(cc + 1) * 1024],
                        in_=fst[:, s2 * 1024:(s2 + 1) * 1024])],
                    waits=[hld, h_dn_pe.get((fq - 1, 15, 1))])
                slot_sem[sl] = hcd
            h_wd_rdy = hcd
            for fc in range(6):
                for tg in range(4):
                    bank_g = 0 + ((fc * 4 + tg) % 2)
                    bank_u = 2 + ((fc * 4 + tg) % 2)
                    fns_g = []
                    fns_u = []
                    for ch in range(8):
                        fns_g.append(lambda t, cc=ch, ff=fc, gg=tg, bb=bank_g: t.matmul(
                            out=self.P[bb][:, :],
                            lhsT=wgq[:, cc * 768 + ff * 128: cc * 768 + (ff + 1) * 128],
                            rhs=xn1[:, cc * 2048 + gg * 512: cc * 2048 + (gg + 1) * 512],
                            start=(cc == 0), stop=(cc == 7)))
                        fns_u.append(lambda t, cc=ch, ff=fc, gg=tg, bb=bank_u: t.matmul(
                            out=self.P[bb][:, :],
                            lhsT=wuq[:, cc * 768 + ff * 128: cc * 768 + (ff + 1) * 128],
                            rhs=xn1[:, cc * 2048 + gg * 512: cc * 2048 + (gg + 1) * 512],
                            start=(cc == 0), stop=(cc == 7)))
                    if fc == 0 and tg == 0:
                        for cc2 in range(8):
                            hg = p.batch("pe", [fns_g[cc2]], waits=(
                                ([h_xn1, self.h_pbank.get(("fg", bank_g))]
                                 if cc2 == 0 else []) + [h_cast_g[(0, cc2)]]))
                        for cc2 in range(8):
                            hu = p.batch("pe", [fns_u[cc2]], waits=(
                                ([self.h_pbank.get(("fu", bank_u))]
                                 if cc2 == 0 else []) + [h_cast_g[(1, cc2)]]))
                    else:
                        hg = p.batch("pe", fns_g, waits=[h_gu_rdy, h_xn1,
                                                         self.h_pbank.get(("fg", bank_g))])
                        hu = p.batch("pe", fns_u, waits=[self.h_pbank.get(("fu", bank_u))])
                    h_gu_pe[(fq, 0, 9)] = hg
                    h_gu_pe[(fq, 1, 9)] = hu
                    ys = (fc * 4 + tg) % 2
                    hs = p.batch("act", [lambda a, bb=bank_g, y2=ys: a.activation(
                        out=sil[:, y2 * 512:(y2 + 1) * 512], in_=self.P[bb][:, :],
                        func=AF.Silu, scale=1.0 / 127.0)],
                        waits=[hg, h_hq.get((fq, fc * 4 + tg - 2))])
                    h_hq[(fq, fc * 4 + tg)] = p.batch("dve", [
                        lambda v, bb=bank_u, y2=ys, ff=fc, gg=tg: v.scalar_tensor_tensor(
                            out=hq[:, ff * 2048 + gg * 512: ff * 2048 + (gg + 1) * 512],
                            in0=self.P[bb][:, :], scalar=1.0 / 127.0,
                            in1=sil[:, y2 * 512:(y2 + 1) * 512],
                            op0=ALU.mult, op1=ALU.mult)],
                        waits=[hs, hu, h_dn_pe.get((fq - 1, 15, 1))])
                    self.h_pbank[("fg", bank_g)] = hs
                    self.h_pbank[("fu", bank_u)] = h_hq[(fq, fc * 4 + tg)]
            h_h_all = [h_hq[(fq, z)] for z in range(24)]
            for j in range(16):
                for mo in range(2):
                    bank = 4 + ((j * 2 + mo) % 2)
                    fns = []
                    for fc in range(6):
                        fns.append(lambda t, cc=fc, jo=j, mm=mo, bb=bank: t.matmul(
                            out=self.P[bb][:, :],
                            lhsT=hq[:, cc * 2048 + jo * 128: cc * 2048 + (jo + 1) * 128],
                            rhs=wdq[:, cc * 1024 + mm * 512: cc * 1024 + (mm + 1) * 512],
                            start=(cc == 0), stop=(cc == 5)))
                    hdp = p.batch("pe", fns, waits=[h_wd_rdy,
                                                    self.h_pbank.get(("fd", bank))] + h_h_all)
                    h_dn_pe[(fq, j, mo)] = hdp
                    h_add[(fq, j, mo)] = p.batch("dve", [
                        lambda v, jj=j, mm=mo, bb=bank: v.scalar_tensor_tensor(
                            out=self.x_tm[:, jj * 1024 + mm * 512: jj * 1024 + (mm + 1) * 512],
                            in0=self.P[bb][:, :], scalar=1.0 / 127.0,
                            in1=self.x_tm[:, jj * 1024 + mm * 512: jj * 1024 + (mm + 1) * 512],
                            op0=ALU.mult, op1=ALU.add)],
                        waits=[hdp, h_xadd, self.h_xtm_dma, h_add.get((fq - 1, j, mo))])
                    self.h_pbank[("fd", bank)] = h_add[(fq, j, mo)]
        self.h_xtm = p.batch("dve", [lambda v: v.nop()],
                             waits=[h_add[(3, j, mo)] for j in range(16) for mo in range(2)])

    def phase_g(self):
        p, d = self.p, self.dram
        src = bass.AP(self.x_tm, 0, [[NOWN * D, 128], [D, 16], [1, D]])
        dst = d["out_own"][:, :].rearrange("(j p) d -> p j d", j=16, p=128)
        p.dma("out", [lambda sp: sp.dma_start(out=dst, in_=src)],
              waits=[self.h_xtm])

    def finish(self):
        self.p.final_wait()
        sems_ctx = []
        nc = self.nc
        sems = {}
        for name in ENGS:
            sems[name] = nc.alloc_semaphore(name=f"sem_{name}")
        for dsem in self.p.ndma:
            sems["dma:" + dsem] = nc.alloc_semaphore(name=f"dsem_{dsem}")
        with nc.Block() as block:
            self.p.realize(block, sems)
        return nc


def build_nc(phases, debug_outputs=()):
    b = Builder(phases, debug_outputs)
    b.load_consts()
    for ph in phases:
        if ph == "A":
            b.phase_a()
        elif ph == "G":
            b.phase_g()
        else:
            getattr(b, f"phase_{ph.lower()}")()
    b.finish()
    return b


def kernel(**inputs):
    in_maps = host_shards(inputs)
    results = [dict() for _ in range(8)]
    for phases in LAUNCH_PLAN:
        b = build_nc(phases)
        lm = []
        for c in range(8):
            mm = {}
            for name, kind in b.kinds.items():
                if kind != "ExternalInput":
                    continue
                if name in in_maps[c]:
                    mm[name] = in_maps[c][name]
                elif name in results[c]:
                    mm[name] = results[c][name]
                else:
                    raise KeyError(f"missing input {name}")
            lm.append(mm)
        res = run_bass_kernel_spmd(b.nc, lm, core_ids=list(range(8)))
        for c in range(8):
            results[c].update(res.results[c])
    return host_gather(results)
